# version 19
# baseline (speedup 1.0000x reference)
import contextlib
import numpy as np
import concourse.bass as bass
import concourse.mybir as mybir
from concourse.bass_utils import run_bass_kernel_spmd
from concourse.alu_op_type import AluOpType as OP

F32 = mybir.dt.float32
BF16 = mybir.dt.bfloat16
I32 = mybir.dt.int32
AF = mybir.ActivationFunctionType
AX = mybir.AxisListType.X

ENGINES = ("sync", "gpsimd", "tensor", "scalar", "vector")

D = 2048
T = 2048
KC = 16
H = 5632
JC = 44
NW = float(H * D)
DBG = False


class Prog:
    def __init__(self):
        self.ops = {e: [] for e in ENGINES}
        self.cnt = {}
        self.lastw = {e: {} for e in ENGINES}

    def op(self, eng, fn, wait=(), inc=None, n=1):
        wl = []
        for t in wait:
            if t is None:
                continue
            s, v = t
            if self.lastw[eng].get(s, 0) < v:
                wl.append((s, v))
                self.lastw[eng][s] = v
        tok = None
        if inc is not None:
            self.cnt[inc] = self.cnt.get(inc, 0) + n
            tok = (inc, self.cnt[inc])
        self.ops[eng].append((wl, fn, inc, n))
        return tok

    def barrier(self):
        toks = list(self.cnt.items())
        for e in ENGINES:
            wl = []
            for s, v in toks:
                if self.lastw[e].get(s, 0) < v:
                    wl.append((s, v))
                    self.lastw[e][s] = v
            if wl:
                self.ops[e].append((wl, None, None, 0))


def dma(dst, src):
    return lambda e: e.dma_start(out=dst, in_=src)


def build():
    nc = bass.Bass("TRN2", target_bir_lowering=False)
    P = Prog()

    xfm = nc.dram_tensor("xfm", [D, T], F32, kind="ExternalInput").ap()
    wq_blk = nc.dram_tensor("wq_blk", [16, 128, 2048], F32, kind="ExternalInput").ap()
    wkv_blk = nc.dram_tensor("wkv_blk", [8, 16, 128, 512], F32, kind="ExternalInput").ap()
    wpj_blk = nc.dram_tensor("wpj_blk", [16, 128, 2048], F32, kind="ExternalInput").ap()
    w1_blk = nc.dram_tensor("w1_blk", [44, 128, 2048], F32, kind="ExternalInput").ap()
    w2_blk = nc.dram_tensor("w2_blk", [44, 128, 2048], F32, kind="ExternalInput").ap()
    w3_blk = nc.dram_tensor("w3_blk", [16, 128, 5632], F32, kind="ExternalInput").ap()
    bcol_in = nc.dram_tensor("bcol", [128, 16], F32, kind="ExternalInput").ap()
    ones_stat_in = nc.dram_tensor("ones_stat", [128, 128], F32, kind="ExternalInput").ap()
    ones_col_in = nc.dram_tensor("ones_col", [128, 1], F32, kind="ExternalInput").ap()
    onesrow_in = nc.dram_tensor("onesrow", [1, 128], F32, kind="ExternalInput").ap()
    mask_in = nc.dram_tensor("mask_bd", [128, 128], F32, kind="ExternalInput").ap()
    epss_in = nc.dram_tensor("epss", [128, 2], F32, kind="ExternalInput").ap()

    y = nc.dram_tensor("y", [D, T], F32, kind="ExternalOutput").ap()
    sk = dict(kind="ExternalOutput") if DBG else {}
    xp_scr = nc.dram_tensor("xp_scr", [16, 128, T], F32, **sk).ap()
    xl2_scr = nc.dram_tensor("xl2_scr", [16, 128, T], BF16, **sk).ap()
    w1q_scr = nc.dram_tensor("w1q_scr", [44, 128, 2048], BF16, **sk).ap()
    w2q_scr = nc.dram_tensor("w2q_scr", [44, 128, 2048], BF16, **sk).ap()
    w3q_scr = nc.dram_tensor("w3q_scr", [16, 128, 5632], BF16, **sk).ap()
    if DBG:
        xln_dbg = nc.dram_tensor("xln_dbg", [16, 128, T], BF16, kind="ExternalOutput").ap()
        vk_dbg = nc.dram_tensor("vk_dbg", [16, 128, 128], BF16, kind="ExternalOutput").ap()
        ks_dbg = nc.dram_tensor("ks_dbg", [16, 128, 128], BF16, kind="ExternalOutput").ap()
        attn_dbg = nc.dram_tensor("attn_dbg", [2, 16, 128, 1024], BF16,
                                  kind="ExternalOutput").ap()
        scl_dbg = nc.dram_tensor("scl_dbg", [128, 4], F32, kind="ExternalOutput").ap()
        hid_dbg = nc.dram_tensor("hid_dbg", [44, 128, 1024], BF16, kind="ExternalOutput").ap()

    es = contextlib.ExitStack()
    sb = lambda name, shape, dt: es.enter_context(nc.sbuf_tensor(name, shape, dt))

    MASK = sb("MASK", [128, 128], F32)
    ONES_ST = sb("ONES_ST", [128, 128], BF16)
    ONES_CB = sb("ONES_CB", [128, 1], BF16)
    ONES_CF = sb("ONES_CF", [128, 1], F32)
    ONESROW = sb("ONESROW", [1, 128], F32)
    GACC = sb("GACC", [128, 272], F32)
    SCLA = sb("SCLA", [1, 4], F32)
    SCLB = sb("SCLB", [1, 4], F32)
    S_COL = sb("S_COL", [128, 4], F32)
    EPS_SB = sb("EPS_SB", [128, 2], F32)

    ab_es = contextlib.ExitStack()
    sba = lambda name, shape, dt: ab_es.enter_context(nc.sbuf_tensor(name, shape, dt))
    XLN = [sba(f"XLN{i}", [128, T], BF16) for i in range(KC)]
    VKBLK = [sba(f"VKB{i}", [128, 128], BF16) for i in range(16)]
    KSBLK = [sba(f"KSB{i}", [128, 128], BF16) for i in range(16)]
    B_COL = sba("B_COL", [128, 16], F32)
    MEAN_SB = sba("MEAN_SB", [128, 1024], F32)
    M2_SB = sba("M2_SB", [128, 1024], F32)
    VAR_SB = sba("VAR_SB", [128, 1024], F32)
    STD_SB = sba("STD_SB", [128, 1024], F32)
    RSTD_SB = sba("RSTD_SB", [128, 1024], F32)
    TMPF = sba("TMPF", [128, 1024], F32)
    GSUM = sba("GSUM", [128, 4], F32)

    P.op("sync", dma(MASK[:], mask_in), inc="sc", n=16)
    P.op("sync", dma(B_COL[:], bcol_in), inc="sc", n=16)
    P.op("sync", dma(ONESROW[:], onesrow_in), inc="sc", n=16)
    P.op("sync", dma(ONES_CF[:], ones_col_in), inc="sc", n=16)
    t_cs = P.op("sync", dma(EPS_SB[:], epss_in), inc="sc", n=16)
    P.op("gpsimd", dma(ONES_ST[:], ones_stat_in), inc="gc", n=16)
    t_cg = P.op("gpsimd", dma(ONES_CB[:], ones_col_in), inc="gc", n=16)

    a_es = contextlib.ExitStack()
    sbA = lambda name, shape, dt: a_es.enter_context(nc.sbuf_tensor(name, shape, dt))
    XTS = [sbA(f"XTS{i}", [128, 512], F32) for i in range(KC)]
    XB = [sbA(f"XB{i}", [128, 512], BF16) for i in range(2)]
    XSQ = [sbA(f"XSQ{i}", [128, 512], BF16) for i in range(2)]
    WKV = [sbA(f"WKV{i}", [128, 16 * 512], BF16) for i in range(2)]
    WKVF = [sbA(f"WKVF{i}", [128, 2048], F32) for i in range(2)]
    GBUF = [sbA(f"GBUF{i}", [128, 1024], F32) for i in range(4)]
    KT = [sbA(f"KT{i}", [128, 256], BF16) for i in range(3)]
    VT = [sbA(f"VT{i}", [128, 256], BF16) for i in range(3)]
    KS_SB = sbA("KS_SB", [128, 16], F32)

    gsteps = []
    for jc in range(44):
        for c0 in (0, 1024):
            gsteps.append((w1_blk[jc][:, c0:c0 + 1024], 1024))
    for jc in range(44):
        for c0 in (0, 1024):
            gsteps.append((w2_blk[jc][:, c0:c0 + 1024], 1024))
    for fc in range(16):
        for c0 in range(0, 5632, 1024):
            w = min(1024, 5632 - c0)
            gsteps.append((w3_blk[fc][:, c0:c0 + w], w))
    n_gsteps = len(gsteps)
    g_state = {"i": 0, "acttok": {}}

    def emit_gamma_step():
        i = g_state["i"]
        if i >= n_gsteps:
            return
        g_state["i"] = i + 1
        src, w = gsteps[i]
        t_ld = P.op("sync", dma(GBUF[i % 4][:, :w], src),
                    wait=[g_state["acttok"].get(i - 4)], inc=f"sg{i % 4}", n=16)

        def f(e, i=i, w=w):
            return e.tensor_reduce(out=GACC[:, i:i + 1], in_=GBUF[i % 4][:, :w],
                                   axis=AX, op=OP.add, apply_absolute_value=True)
        g_state["acttok"][i] = P.op("vector", f, wait=[t_ld], inc="dve")

    kvmm_tok = {}
    wkv_tok = {}
    wkvf_cast = {}
    wkv_ld = {}

    def emit_wkv_load(gg, gi):
        i = gg * 4 + gi
        k0 = 4 * gi
        srcap = wkv_blk[gg, k0:k0 + 4].rearrange("k p f -> p k f")
        stg = WKVF[i % 2][:].rearrange("p (k f) -> p k f", k=4)
        wkv_ld[i] = P.op("gpsimd", dma(stg, srcap), wait=[wkvf_cast.get(i - 2)],
                         inc=f"skv{i % 2}", n=16)

    def emit_wkv_cast(gg, gi):
        i = gg * 4 + gi
        k0 = 4 * gi
        par = gg % 2
        wkvf_cast[i] = P.op("scalar",
                            lambda e, par=par, k0=k0, i=i: e.activation(
                                out=WKV[par][:, k0 * 512:(k0 + 4) * 512],
                                in_=WKVF[i % 2][:], func=AF.Copy),
                            wait=[wkv_ld[i], kvmm_tok.get((gg - 2, 15))], inc="act")
        if gi == 3:
            wkv_tok[gg] = wkvf_cast[i]

    emit_wkv_load(0, 0)
    emit_wkv_load(0, 1)

    a1_ps = contextlib.ExitStack()
    MEAN_PS = a1_ps.enter_context(nc.psum_tensor("MEAN_PS", [128, 512], F32))
    MSQ_PS = a1_ps.enter_context(nc.psum_tensor("MSQ_PS", [128, 512], F32))

    apply_tok = {}
    xln_tok = {}
    mm_stat_tok = {}
    psum_free_tok = {}
    for ts in range(4):
        c0 = ts * 512
        txs = {}
        for kc in range(KC):
            txs[kc] = P.op("sync", dma(XTS[kc][:], xfm[kc * 128:(kc + 1) * 128, c0:c0 + 512]),
                           wait=[apply_tok.get((ts - 1, kc))], inc="sx", n=16)
        cast_tok = {}
        for kc in range(KC):
            cast_tok[(kc, 0)] = P.op(
                "scalar",
                lambda e, kc=kc: e.activation(out=XB[kc % 2][:], in_=XTS[kc][:], func=AF.Copy),
                wait=[txs[kc], mm_stat_tok.get((ts, kc - 2, 0))], inc="act")
            cast_tok[(kc, 1)] = P.op(
                "scalar",
                lambda e, kc=kc: e.activation(out=XSQ[kc % 2][:], in_=XTS[kc][:],
                                              func=AF.Square),
                wait=[mm_stat_tok.get((ts, kc - 2, 1))], inc="act")
            war_ps = psum_free_tok.get(ts - 1) if kc == 0 else None
            mm_stat_tok[(ts, kc, 0)] = P.op(
                "tensor",
                lambda e, kc=kc: e.matmul(MEAN_PS[:], ONES_ST[:], XB[kc % 2][:],
                                          start=(kc == 0), stop=(kc == KC - 1)),
                wait=[cast_tok[(kc, 0)], t_cg, war_ps], inc="pe")
            mm_stat_tok[(ts, kc, 1)] = P.op(
                "tensor",
                lambda e, kc=kc: e.matmul(MSQ_PS[:], ONES_ST[:], XSQ[kc % 2][:],
                                          start=(kc == 0), stop=(kc == KC - 1)),
                wait=[cast_tok[(kc, 1)]], inc="pe")
        emit_wkv_cast(0, ts)
        if ts < 2:
            emit_wkv_load(0, ts + 2)
        else:
            emit_wkv_load(1, ts - 2)
        t1 = P.op("vector", lambda e: e.tensor_copy(out=MEAN_SB[:, :512], in_=MEAN_PS[:]),
                  wait=[mm_stat_tok[(ts, KC - 1, 0)]], inc="dve")
        t2 = P.op("scalar", lambda e: e.activation(out=M2_SB[:, :512], in_=MEAN_SB[:, :512],
                                                   func=AF.Square), wait=[t1], inc="act")
        t3 = P.op("vector", lambda e: e.tensor_tensor(out=VAR_SB[:, :512], in0=MSQ_PS[:],
                                                      in1=M2_SB[:, :512], op=OP.subtract),
                  wait=[mm_stat_tok[(ts, KC - 1, 1)], t2], inc="dve")
        psum_free_tok[ts] = t3
        t4 = P.op("scalar", lambda e: e.activation(out=STD_SB[:, :512], in_=VAR_SB[:, :512],
                                                   func=AF.Ln, bias=EPS_SB[:, 0:1]),
                  wait=[t3, t_cs], inc="act")
        t5 = P.op("scalar", lambda e: e.activation(out=RSTD_SB[:, :512], in_=STD_SB[:, :512],
                                                   func=AF.Exp, scale=-0.5),
                  wait=[t4], inc="act")
        for kc in range(KC):
            P.op("vector",
                 lambda e, kc=kc: e.tensor_tensor(out=TMPF[:, :512], in0=XTS[kc][:],
                                                  in1=MEAN_SB[:, :512], op=OP.subtract),
                 wait=[t5], inc="dve")
            tok = P.op("vector",
                       lambda e, kc=kc, c0=c0: e.tensor_tensor(
                           out=XLN[kc][:, c0:c0 + 512], in0=TMPF[:, :512],
                           in1=RSTD_SB[:, :512], op=OP.mult), inc="dve")
            apply_tok[(ts, kc)] = tok
            xln_tok[(ts, kc)] = tok
        for _ in range(4):
            emit_gamma_step()

    a1_ps.close()

    a2_ps = contextlib.ExitStack()
    KVPS = [a2_ps.enter_context(nc.psum_tensor(f"KVPS{i}", [128, 512], F32)) for i in range(3)]
    VKPS = a2_ps.enter_context(nc.psum_tensor("VKPS", [128, 2048], F32))
    KSPS = a2_ps.enter_context(nc.psum_tensor("KSPS", [128, 512], F32))

    def prev_r_use(gg, tt):
        if tt >= 3:
            return (gg, tt - 3)
        if gg == 0:
            return None
        return (gg - 1, 15 if tt == 0 else tt + 12)

    evac_tok = {}
    eins_tok = {}
    pend_eins = None

    def emit_eins(gg, tt, r):
        for g01 in range(2):
            g = 2 * gg + g01
            P.op("tensor",
                 lambda e, r=r, g01=g01, g=g, tt=tt: e.matmul(
                     VKPS[:, g * 128:(g + 1) * 128], KT[r][:, g01 * 128:(g01 + 1) * 128],
                     VT[r][:, g01 * 128:(g01 + 1) * 128],
                     start=(tt == 0 and g % 4 == 0), stop=(tt == 15)),
                 wait=[evac_tok[(gg, tt)], t_cg], inc="pe")
            eins_tok[(gg, tt)] = P.op(
                "tensor",
                lambda e, r=r, g01=g01, g=g, tt=tt: e.matmul(
                    KSPS[:, g:g + 1], KT[r][:, g01 * 128:(g01 + 1) * 128], ONES_CB[:],
                    start=(tt == 0 and g == 0), stop=(tt == 15)),
                inc="pe")

    for gg in range(8):
        par = gg % 2
        for tt in range(16):
            r = tt % 3
            if gg < 7:
                if tt == 0:
                    emit_wkv_load(gg + 1, 0)
                elif tt == 1:
                    emit_wkv_load(gg + 1, 1)
                elif tt == 4:
                    emit_wkv_cast(gg + 1, 0)
                elif tt == 5:
                    emit_wkv_load(gg + 1, 2)
                elif tt == 7:
                    emit_wkv_cast(gg + 1, 1)
                elif tt == 8:
                    emit_wkv_load(gg + 1, 3)
                elif tt == 10:
                    emit_wkv_cast(gg + 1, 2)
                elif tt == 13:
                    emit_wkv_cast(gg + 1, 3)
            for kc in range(KC):
                w = [xln_tok[(tt // 4, kc)], wkv_tok[gg]]
                if kc == 0:
                    w.append(evac_tok.get(prev_r_use(gg, tt)))
                    if gg == 0 and tt < 3:
                        w.append(psum_free_tok[3])
                kvmm_tok[(gg, tt)] = P.op(
                    "tensor",
                    lambda e, kc=kc, r=r, tt=tt, par=par: e.matmul(
                        KVPS[r][:], XLN[kc][:, tt * 128:(tt + 1) * 128],
                        WKV[par][:, kc * 512:(kc + 1) * 512],
                        start=(kc == 0), stop=(kc == KC - 1)),
                    wait=w, inc="pe")
            if pend_eins is not None:
                emit_eins(*pend_eins)
            wE = [kvmm_tok[(gg, tt)], eins_tok.get(prev_r_use(gg, tt))]
            P.op("scalar", lambda e, r=r: e.activation(out=KT[r][:], in_=KVPS[r][:, 0:256],
                                                       func=AF.Relu), wait=wE, inc="act")
            evac_tok[(gg, tt)] = P.op(
                "scalar", lambda e, r=r: e.activation(out=VT[r][:], in_=KVPS[r][:, 256:512],
                                                      func=AF.Copy), inc="act")
            pend_eins = (gg, tt, r)
            emit_gamma_step()
            if tt % 2 == 0:
                emit_gamma_step()
    emit_eins(*pend_eins)

    vk_done = eins_tok[(7, 15)]

    t_ks = P.op("vector", lambda e: e.tensor_copy(out=KS_SB[:], in_=KSPS[:, 0:16]),
                wait=[vk_done, t_cs], inc="dve")
    vkblk_tok = {}
    for g in range(16):
        P.op("vector",
             lambda e, g=g: e.tensor_tensor(out=VKBLK[g][:],
                                            in0=VKPS[:, g * 128:(g + 1) * 128],
                                            in1=MASK[:], op=OP.mult), inc="dve")
        vkblk_tok[g] = P.op("vector",
                            lambda e, g=g: e.tensor_scalar(out=KSBLK[g][:], in0=MASK[:],
                                                           scalar1=KS_SB[:, g:g + 1],
                                                           scalar2=None, op0=OP.mult),
                            inc="dve")
    if DBG:
        for g in range(16):
            P.op("gpsimd", dma(vk_dbg[g], VKBLK[g][:]), wait=[vkblk_tok[g]], inc="gs", n=16)
            P.op("gpsimd", dma(ks_dbg[g], KSBLK[g][:]), inc="gs", n=16)
        for kc in range(KC):
            P.op("gpsimd", dma(xln_dbg[kc], XLN[kc][:]), inc="gs", n=16)
    a2_ps.close()

    P.barrier()
    a_es.close()

    b_es = contextlib.ExitStack()
    sbB = lambda name, shape, dt: b_es.enter_context(nc.sbuf_tensor(name, shape, dt))
    ATTN = [sbB(f"ATTN{i}", [128, 1024], BF16) for i in range(16)]
    WQS = [sbB(f"WQS{i}", [128, 2048], BF16) for i in range(2)]
    WPS = [sbB(f"WPS{i}", [128, 2048], BF16) for i in range(2)]
    QSB = [sbB(f"QSB{i}", [128, 1024], BF16) for i in range(2)]
    DEN_SB = sbB("DEN_SB", [128, 1024], F32)
    RCP_SB = sbB("RCP_SB", [128, 1024], F32)
    XRES = [sbB(f"XRES{i}", [128, 1024], F32) for i in range(2)]
    XP_SB = [sbB(f"XP{i}", [128, 1024], F32) for i in range(2)]
    XB2 = [sbB(f"XB2_{i}", [128, 1024], BF16) for i in range(2)]
    XSQ2 = [sbB(f"XSQ2_{i}", [128, 1024], BF16) for i in range(2)]
    XL2 = [sbB(f"XL2_{i}", [128, 1024], BF16) for i in range(2)]
    WSTG = [sbB(f"WSTG{i}", [128, 2048], F32) for i in range(2)]

    xp_store_tok = {}
    gb_lastload = {}

    def emit_gamma_step_b(extra=(), pair=None):
        i = g_state["i"]
        if i >= n_gsteps:
            return
        g_state["i"] = i + 1
        src, w = gsteps[i]
        p2 = i % 2
        stg = pair[p2]
        lw = [g_state["acttok"].get(i - 2)] + list(extra)
        t_ld = P.op("gpsimd", dma(stg[:, 0:w], src), wait=lw, inc=f"gb{p2}", n=16)
        gb_lastload[p2] = t_ld

        def f(e, i=i, w=w, stg=stg):
            return e.tensor_reduce(out=GACC[:, i:i + 1], in_=stg[:, 0:w],
                                   axis=AX, op=OP.add, apply_absolute_value=True)
        g_state["acttok"][i] = P.op("vector", f, wait=[t_ld], inc="dve")

    def emit_apply_chunk(actx):
        if actx is None:
            return
        kc = actx["kc"]
        if kc >= KC:
            return
        actx["kc"] = kc + 1
        th_, h0_, nL = actx["th"], actx["h0"], actx["nL"]
        s = kc % nL
        par = kc % 2
        P.op("vector",
             lambda e, actx=actx, s=s: e.tensor_tensor(out=TMPF[:], in0=actx["rslot"][s],
                                                       in1=MEAN_SB[:], op=OP.subtract),
             wait=[actx["xpb"][kc], actx["t5"], actx["xl2st"].get(kc - 2)], inc="dve")
        actx["xl2"][kc] = P.op("vector",
                               lambda e, par=par: e.tensor_tensor(out=XL2[par][:],
                                                                  in0=TMPF[:],
                                                                  in1=RSTD_SB[:],
                                                                  op=OP.mult),
                               inc="dve")
        if kc + nL < KC:
            s2 = (kc + nL) % nL
            t_l = P.op("gpsimd",
                       dma(actx["rslot"][s2], xp_scr[kc + nL][:, h0_:h0_ + 1024]),
                       wait=[xp_store_tok[(th_, kc + nL)], actx["xl2"][kc]],
                       inc=f"bx{th_}{s2}", n=16)
            actx["xpb"][kc + nL] = t_l
            if s2 == 4:
                actx["var_last_ld"] = t_l
        actx["xl2st"][kc] = P.op("gpsimd",
                                 dma(xl2_scr[kc][:, h0_:h0_ + 1024], XL2[par][:]),
                                 wait=[actx["xl2"][kc]], inc=f"gx{par}", n=16)

    prev_th = {}
    prev_actx = None
    for th in range(2):
        h0 = th * 1024
        pa = prev_actx
        if th == 0:
            gb_pair = (TMPF, MEAN_SB)
            gb_gates = (None, None)
        else:
            gb_pair = (M2_SB, STD_SB)
            gb_gates = (prev_th.get("t3"), prev_th.get("t5"))
        gb_cnt = [0]
        bq_ps = contextlib.ExitStack()
        QPS = [bq_ps.enter_context(nc.psum_tensor(f"QPS{th}{i}", [128, 1024], F32))
               for i in range(2)]
        NUMPS = bq_ps.enter_context(nc.psum_tensor(f"NUMPS{th}", [128, 1024], F32))
        DENPS = bq_ps.enter_context(nc.psum_tensor(f"DENPS{th}", [128, 1024], F32))
        wqs_tok = {}
        qmm_tok = {}
        relu_tok = {}
        nd_tok = {}
        den_tok = {}
        rcp_tok = {}
        attn_tok = {}

        def emit_numden(qc, attn_tok=attn_tok, relu_tok=relu_tok, nd_tok=nd_tok,
                        den_tok=den_tok, rcp_tok=rcp_tok, NUMPS=NUMPS, DENPS=DENPS):
            par = qc % 2
            for h2 in range(2):
                wn = [relu_tok[qc], attn_tok.get(qc - 1), vkblk_tok[15]]
                if qc == 0:
                    wn.append(prev_th.get("stats_free"))
                P.op("tensor",
                     lambda e, qc=qc, h2=h2, par=par: e.matmul(
                         NUMPS[:, h2 * 512:(h2 + 1) * 512], VKBLK[qc][:],
                         QSB[par][:, h2 * 512:(h2 + 1) * 512], start=True, stop=True),
                     wait=wn, inc="pe")
                nd_tok[qc] = P.op(
                    "tensor",
                    lambda e, qc=qc, h2=h2, par=par: e.matmul(
                        DENPS[:, h2 * 512:(h2 + 1) * 512], KSBLK[qc][:],
                        QSB[par][:, h2 * 512:(h2 + 1) * 512], start=True, stop=True),
                    wait=[den_tok.get(qc - 1)], inc="pe")
            den_tok[qc] = P.op("scalar",
                               lambda e: e.activation(out=DEN_SB[:], in_=DENPS[:],
                                                      func=AF.Ln, bias=EPS_SB[:, 1:2]),
                               wait=[nd_tok[qc], rcp_tok.get(qc - 1), t_cs], inc="act")
            rcp_tok[qc] = P.op("scalar",
                               lambda e: e.activation(out=RCP_SB[:], in_=DEN_SB[:],
                                                      func=AF.Exp, scale=-1.0),
                               wait=[attn_tok.get(qc - 1)], inc="act")
            attn_tok[qc] = P.op("vector",
                                lambda e, qc=qc: e.tensor_tensor(out=ATTN[qc][:], in0=NUMPS[:],
                                                                 in1=RCP_SB[:], op=OP.mult),
                                wait=[rcp_tok[qc]], inc="dve")

        def emit_wq_load(qc, wqs_tok=wqs_tok, qmm_tok=qmm_tok, wstg_prev=None,
                         twq_pre=None):
            par = qc % 2
            if twq_pre is not None:
                twq = twq_pre
            else:
                twq = P.op("sync", dma(WSTG[par][:], wq_blk[qc]),
                           wait=[wqs_tok.get(qc - 2), wstg_prev], inc=f"sw{par}", n=16)
            wqs_tok[qc] = P.op("scalar",
                               lambda e, par=par: e.activation(out=WQS[par][:],
                                                               in_=WSTG[par][:],
                                                               func=AF.Copy),
                               wait=[twq, qmm_tok.get(qc - 2),
                                     prev_th.get("qps_free")], inc="act")

        emit_wq_load(0, wstg_prev=prev_th.get("wstg0"),
                     twq_pre=prev_th.get("pre_wq0"))
        emit_wq_load(1, wstg_prev=prev_th.get("wstg1"),
                     twq_pre=prev_th.get("pre_wq1"))
        for qc in range(16):
            par = qc % 2
            for kc in range(KC):
                for h2 in range(2):
                    wq = [wqs_tok[qc], relu_tok.get(qc - 2)]
                    if qc == 0 and kc == 0:
                        wq.append(prev_th.get("pps_free"))
                        wq.append(prev_th.get("stats_free"))
                    qmm_tok[qc] = P.op(
                        "tensor",
                        lambda e, kc=kc, h2=h2, par=par, h0=h0: e.matmul(
                            QPS[par][:, h2 * 512:(h2 + 1) * 512],
                            WQS[par][:, kc * 128:(kc + 1) * 128],
                            XLN[kc][:, h0 + h2 * 512:h0 + (h2 + 1) * 512],
                            start=(kc == 0), stop=(kc == KC - 1)),
                        wait=wq, inc="pe")
            if qc + 1 < 16:
                emit_wq_load(qc + 1)
            if qc > 0:
                emit_numden(qc - 1)
            emit_apply_chunk(pa)
            emit_apply_chunk(pa)
            if th == 0:
                for _g in range(4):
                    ex = [gb_gates[gb_cnt[0] % 2]] if gb_cnt[0] < 2 else []
                    gb_cnt[0] += 1
                    emit_gamma_step_b(ex, gb_pair)
            relu_tok[qc] = P.op("scalar",
                                lambda e, par=par: e.activation(out=QSB[par][:],
                                                                in_=QPS[par][:], func=AF.Relu),
                                wait=[qmm_tok[qc], nd_tok.get(qc - 2)], inc="act")
        emit_numden(15)
        if DBG:
            for qc in range(16):
                P.op("gpsimd", dma(attn_dbg[th, qc], ATTN[qc][:]), wait=[attn_tok[qc]],
                     inc="gs", n=16)
        bq_ps.close()

        bp_ps = contextlib.ExitStack()
        PPS = [bp_ps.enter_context(nc.psum_tensor(f"PPS{th}{i}", [128, 1024], F32))
               for i in range(2)]
        MEAN2_PS = bp_ps.enter_context(nc.psum_tensor(f"MEAN2_{th}", [128, 1024], F32))
        MSQ2_PS = bp_ps.enter_context(nc.psum_tensor(f"MSQ2_{th}", [128, 1024], F32))
        wps_tok = {}
        pmm_tok = {}
        xres_tok = {}
        xp_tok = {}
        cast2_tok = {}
        smm_tok = {}

        def emit_stats(oc, cast2_tok=cast2_tok, smm_tok=smm_tok,
                       MEAN2_PS=MEAN2_PS, MSQ2_PS=MSQ2_PS,
                       attn_tok=attn_tok, den_tok=den_tok, rcp_tok=rcp_tok):
            par = oc % 2
            for h2 in range(2):
                ws = [cast2_tok[(oc, 0)]]
                if oc == 0:
                    ws.extend([attn_tok[15], den_tok[15], rcp_tok[15]])
                smm_tok[(oc, 0)] = P.op(
                    "tensor",
                    lambda e, h2=h2, oc=oc, par=par: e.matmul(
                        MEAN2_PS[:, h2 * 512:(h2 + 1) * 512], ONES_ST[:],
                        XB2[par][:, h2 * 512:(h2 + 1) * 512],
                        start=(oc == 0), stop=(oc == 15)),
                    wait=ws, inc="pe")
                smm_tok[(oc, 1)] = P.op(
                    "tensor",
                    lambda e, h2=h2, oc=oc, par=par: e.matmul(
                        MSQ2_PS[:, h2 * 512:(h2 + 1) * 512], ONES_ST[:],
                        XSQ2[par][:, h2 * 512:(h2 + 1) * 512],
                        start=(oc == 0), stop=(oc == 15)),
                    wait=[cast2_tok[(oc, 1)]], inc="pe")

        def emit_wp_load(oc, wps_tok=wps_tok, pmm_tok=pmm_tok, wqs_tok=wqs_tok):
            par = oc % 2
            twp = P.op("sync", dma(WSTG[par][:], wpj_blk[oc]),
                       wait=[wps_tok.get(oc - 2), wqs_tok[14 + par]], inc=f"sw{par}", n=16)
            wps_tok[oc] = P.op("scalar",
                               lambda e, par=par: e.activation(out=WPS[par][:],
                                                               in_=WSTG[par][:],
                                                               func=AF.Copy),
                               wait=[twp, pmm_tok.get(oc - 2)], inc="act")

        emit_wp_load(0)
        emit_wp_load(1)
        for oc in range(16):
            par = oc % 2
            xres_tok[oc] = P.op("sync",
                                dma(XRES[par][:], xfm[oc * 128:(oc + 1) * 128, h0:h0 + 1024]),
                                wait=[xp_tok.get(oc - 2),
                                      pa["xl2"].get(12 + oc)
                                      if (pa is not None and oc < 2) else None],
                                inc=f"sxr{par}", n=16)
            for ac in range(16):
                for h2 in range(2):
                    wp = [wps_tok[oc], attn_tok[15], xp_tok.get(oc - 2)]
                    if oc == 0 and ac == 0:
                        wp.extend([relu_tok[15], den_tok[15]])
                    pmm_tok[oc] = P.op(
                        "tensor",
                        lambda e, ac=ac, h2=h2, par=par: e.matmul(
                            PPS[par][:, h2 * 512:(h2 + 1) * 512],
                            WPS[par][:, ac * 128:(ac + 1) * 128],
                            ATTN[ac][:, h2 * 512:(h2 + 1) * 512],
                            start=(ac == 0), stop=(ac == 15)),
                        wait=wp, inc="pe")
            if oc + 1 < 16:
                emit_wp_load(oc + 1)
            if oc > 0:
                emit_stats(oc - 1)
            xp_tok[oc] = P.op("vector",
                              lambda e, oc=oc, par=par: e.scalar_tensor_tensor(
                                  out=XP_SB[par][:], in0=PPS[par][:],
                                  scalar=B_COL[:, oc:oc + 1], in1=XRES[par][:],
                                  op0=OP.add, op1=OP.add),
                              wait=[pmm_tok[oc], xres_tok[oc], t_cs,
                                    xp_store_tok.get((th, oc - 2)),
                                    cast2_tok.get((oc - 2, 1))], inc="dve")
            xp_store_tok[(th, oc)] = P.op("gpsimd",
                                          dma(xp_scr[oc][:, h0:h0 + 1024], XP_SB[par][:]),
                                          wait=[xp_tok[oc]], inc="gs", n=16)
            cast2_tok[(oc, 0)] = P.op("scalar",
                                      lambda e, par=par: e.activation(
                                          out=XB2[par][:], in_=XP_SB[par][:], func=AF.Copy),
                                      wait=[xp_tok[oc], smm_tok.get((oc - 2, 0))], inc="act")
            cast2_tok[(oc, 1)] = P.op("scalar",
                                      lambda e, par=par: e.activation(
                                          out=XSQ2[par][:], in_=XP_SB[par][:], func=AF.Square),
                                      wait=[smm_tok.get((oc - 2, 1))], inc="act")
        emit_stats(15)
        t1 = P.op("vector", lambda e: e.tensor_copy(out=MEAN_SB[:], in_=MEAN2_PS[:]),
                  wait=[smm_tok[(15, 0)], gb_lastload.get(0), gb_lastload.get(1)],
                  inc="dve")
        t2 = P.op("scalar", lambda e: e.activation(out=M2_SB[:], in_=MEAN_SB[:],
                                                   func=AF.Square),
                  wait=[t1, gb_lastload.get(0)], inc="act")
        t3 = P.op("vector", lambda e: e.tensor_tensor(out=VAR_SB[:], in0=MSQ2_PS[:],
                                                      in1=M2_SB[:], op=OP.subtract),
                  wait=[smm_tok[(15, 1)], t2,
                        pa.get("var_last_ld") if pa is not None else None], inc="dve")
        t4 = P.op("scalar", lambda e: e.activation(out=STD_SB[:], in_=VAR_SB[:], func=AF.Ln,
                                                   bias=EPS_SB[:, 0:1]),
                  wait=[t3, gb_lastload.get(1)], inc="act")
        t5 = P.op("scalar", lambda e: e.activation(out=RSTD_SB[:], in_=STD_SB[:],
                                                   func=AF.Exp, scale=-0.5),
                  wait=[t4, pa["xl2"].get(15) if pa is not None else None], inc="act")
        pre_wq = {}
        if th == 0:
            for q in range(2):
                pre_wq[q] = P.op("sync", dma(WSTG[q][:], wq_blk[q]),
                                 wait=[wps_tok[14 + q]], inc=f"sw{q}", n=16)
        if th == 0:
            rslot = [XP_SB[0][:], XP_SB[1][:], XRES[0][:], XRES[1][:], VAR_SB[:]]
            rfree = [cast2_tok[(14, 1)], cast2_tok[(15, 1)], xp_tok[15], xp_tok[15], t5]
        else:
            rslot = [XP_SB[0][:], XP_SB[1][:], XRES[0][:], XRES[1][:], VAR_SB[:],
                     DEN_SB[:], RCP_SB[:]]
            rfree = [cast2_tok[(14, 1)], cast2_tok[(15, 1)], xp_tok[15], xp_tok[15], t5,
                     attn_tok[15], attn_tok[15]]
        rfree2 = [xp_store_tok[(th, 14)], xp_store_tok[(th, 15)]]
        actx = {"th": th, "h0": h0, "t5": t5, "kc": 0, "nL": len(rslot),
                "rslot": rslot, "xpb": {}, "xl2": {}, "xl2st": {}}
        if pa is not None:
            actx["xl2st"][-2] = pa["xl2st"].get(14)
            actx["xl2st"][-1] = pa["xl2st"].get(15)
        for s in range(actx["nL"]):
            actx["xpb"][s] = P.op("gpsimd", dma(rslot[s], xp_scr[s][:, h0:h0 + 1024]),
                                  wait=[xp_store_tok[(th, s)], rfree[s],
                                        rfree2[s] if s < 2 else None],
                                  inc=f"bx{th}{s}", n=16)
            if s == 4:
                actx["var_last_ld"] = actx["xpb"][s]
        prev_actx = actx
        prev_th = {
            "pps_free": xp_tok[15],
            "stats_free": t3,
            "qps_free": pmm_tok[15],
            "wstg0": wps_tok[14],
            "wstg1": wps_tok[15],
            "pre_wq0": pre_wq.get(0),
            "pre_wq1": pre_wq.get(1),
            "t3": t3,
            "t5": t5,
        }
        bp_ps.close()

    while prev_actx["kc"] < KC:
        emit_apply_chunk(prev_actx)
    while g_state["i"] < n_gsteps:
        emit_gamma_step_b(pair=(M2_SB, STD_SB))

    gf_ps = contextlib.ExitStack()
    GPS = gf_ps.enter_context(nc.psum_tensor("GPS", [1, 512], F32))
    BPS = gf_ps.enter_context(nc.psum_tensor("BPS", [128, 512], F32))
    g_last = g_state["acttok"][n_gsteps - 1]
    rtoks = []
    for m, (c0, c1) in enumerate(((0, 88), (88, 176), (176, 272))):
        rtoks.append(P.op("vector",
                          lambda e, m=m, c0=c0, c1=c1: e.tensor_reduce(
                              out=GSUM[:, m:m + 1], in_=GACC[:, c0:c1], axis=AX, op=OP.add),
                          wait=[g_last], inc="dve"))
    gmm = []
    for m in range(3):
        gmm.append(P.op("tensor",
                        lambda e, m=m: e.matmul(GPS[:, m:m + 1], GSUM[:, m:m + 1], ONES_CF[:],
                                                start=True, stop=True),
                        wait=[rtoks[m], t_cs, prev_th["pps_free"],
                              prev_th["stats_free"]], inc="pe"))
    tg1 = P.op("vector", lambda e: e.tensor_scalar(out=SCLA[:, 0:3], in0=GPS[0:1, 0:3],
                                                   scalar1=1.0 / NW, scalar2=1e-8,
                                                   op0=OP.mult, op1=OP.add),
               wait=[gmm[2]], inc="dve")
    tg2 = P.op("vector", lambda e: e.reciprocal(out=SCLB[:, 0:3], in_=SCLA[:, 0:3]),
               wait=[tg1], inc="dve")
    tg3 = P.op("tensor", lambda e: e.matmul(BPS[:, 0:3], ONESROW[:], SCLB[:, 0:3],
                                            start=True, stop=True),
               wait=[tg2], inc="pe")
    t_scol = P.op("vector", lambda e: e.tensor_copy(out=S_COL[:, 0:3], in_=BPS[:, 0:3]),
                  wait=[tg3], inc="dve")
    gf_ps.close()

    P.barrier()
    b_es.close()
    ab_es.close()

    f_es = contextlib.ExitStack()
    sbF = lambda name, shape, dt: f_es.enter_context(nc.sbuf_tensor(name, shape, dt))

    HID = [sbF(f"HID{i}", [128, 1024], BF16) for i in range(JC)]
    XLN2T = [sbF(f"XL2T{i}", [128, 1024], BF16) for i in range(KC)]
    W1S = [sbF(f"W1S{i}", [128, 2048], BF16) for i in range(2)]
    W2S = [sbF(f"W2S{i}", [128, 2048], BF16) for i in range(2)]
    SIG = [sbF(f"SIG{i}", [128, 1024], BF16) for i in range(2)]
    QFA = [sbF(f"QFA{i}", [128, 2048], F32) for i in range(3)]
    QIA = sbF("QIA", [128, 2048], I32)
    W3S = [sbF(f"W3S{i}", [128, 5632], BF16) for i in range(2)]
    OUT_SB = [sbF(f"OUT{i}", [128, 1024], F32) for i in range(2)]

    w3_strips = []
    for fc in range(16):
        for c0 in range(0, 5632, 1024):
            w3_strips.append((fc, c0, min(1024, 5632 - c0)))
    w3pq = {"i": 0, "mm": {}, "mult": {}, "asm": {}, "st": {}, "ommref": {},
            "qfa0free": None}

    def emit_w3_step():
        s = w3pq["i"]
        if s >= len(w3_strips):
            return
        w3pq["i"] = s + 1
        fc, c0, w = w3_strips[s]
        p2 = s % 2
        if s < 12:
            dst = OUT_SB[p2][:, 0:w]
        else:
            dst = QFA[0][:, p2 * 1024:p2 * 1024 + w]
        lw = [w3pq["mm"].get(s - 2)]
        if s in (12, 13):
            lw.append(w3pq["qfa0free"])
        t_ld = P.op("sync", dma(dst, w3_blk[fc][:, c0:c0 + w]), wait=lw,
                    inc=f"q3{p2}", n=16)
        mw = [t_ld, t_scol]
        if c0 == 0 and fc >= 2:
            mw.extend([w3pq["ommref"][fc - 2], w3pq["st"][fc - 2]])
        w3pq["mult"][s] = P.op(
            "vector",
            lambda e, dst=dst, w=w: e.tensor_scalar(
                out=QIA[:, 0:w], in0=dst, scalar1=S_COL[:, 2:3], scalar2=None,
                op0=OP.mult),
            wait=mw, inc="dve")
        w3pq["mm"][s] = P.op(
            "vector",
            lambda e, fc=fc, c0=c0, w=w: e.tensor_scalar(
                out=W3S[fc % 2][:, c0:c0 + w], in0=QIA[:, 0:w],
                scalar1=1.0, scalar2=-1.0, op0=OP.min, op1=OP.max),
            inc="dve")
        if c0 == 5120:
            w3pq["asm"][fc] = w3pq["mm"][s]
            w3pq["st"][fc] = P.op("gpsimd", dma(w3q_scr[fc], W3S[fc % 2][:]),
                                  wait=[w3pq["mm"][s]], inc="g3", n=16)

    out_tok = {}
    prev_tile = {}
    for tile in range(2):
        tc0 = tile * 1024
        xl2t = None
        for kc in range(KC):
            xl2t = P.op("sync", dma(XLN2T[kc][:], xl2_scr[kc][:, tc0:tc0 + 1024]),
                        wait=[prev_tile.get("ff143")], inc="sf", n=16)
        xl2t_all = xl2t
        f1_ps = contextlib.ExitStack()
        X1PS = [f1_ps.enter_context(nc.psum_tensor(f"X1PS{tile}{i}", [128, 1024], F32))
                for i in range(2)]
        X2PS = [f1_ps.enter_context(nc.psum_tensor(f"X2PS{tile}{i}", [128, 1024], F32))
                for i in range(2)]
        w1s_tok = {}
        w2s_tok = {}
        w1wr_tok = {}
        w2wr_tok = {}
        ff1_tok = {}
        sig_tok = {}
        hid_tok = {}
        qdve_tok = {}

        def emit_hid(jc, sig_tok=sig_tok, hid_tok=hid_tok, X2PS=X2PS):
            par = jc % 2
            hid_tok[jc] = P.op("vector",
                               lambda e, jc=jc, par=par: e.tensor_tensor(
                                   out=HID[jc][:], in0=SIG[par][:], in1=X2PS[par][:],
                                   op=OP.mult),
                               wait=[sig_tok[jc]], inc="dve")

        for jc in range(JC):
            par = jc % 2
            if tile == 0:
                q1b = par * 2
                tq1 = P.op("sync", dma(QFA[q1b][:], w1_blk[jc]),
                           wait=[qdve_tok.get((jc - 2, 0))], inc=f"qa{par}", n=16)
                tq2 = P.op("sync", dma(QFA[1][:], w2_blk[jc]),
                           wait=[qdve_tok.get((jc - 1, 1))], inc="qb", n=16)
                P.op("vector",
                     lambda e, q1b=q1b: e.tensor_scalar(out=QIA[:], in0=QFA[q1b][:],
                                                        scalar1=S_COL[:, 0:1], scalar2=None,
                                                        op0=OP.mult),
                     wait=[tq1, t_scol, ff1_tok.get(jc - 2), w1wr_tok.get(jc - 2)],
                     inc="dve")
                qdve_tok[(jc, 0)] = P.op(
                    "vector",
                    lambda e, par=par: e.tensor_scalar(out=W1S[par][:], in0=QIA[:],
                                                       scalar1=1.0, scalar2=-1.0,
                                                       op0=OP.min, op1=OP.max), inc="dve")
                w1s_tok[jc] = qdve_tok[(jc, 0)]
                w1wr_tok[jc] = P.op("gpsimd", dma(w1q_scr[jc], W1S[par][:]),
                                    wait=[qdve_tok[(jc, 0)]], inc="gs", n=16)
                P.op("vector",
                     lambda e: e.tensor_scalar(out=QIA[:], in0=QFA[1][:],
                                               scalar1=S_COL[:, 1:2], scalar2=None,
                                               op0=OP.mult),
                     wait=[tq2, w2wr_tok.get(jc - 2)], inc="dve")
                qdve_tok[(jc, 1)] = P.op(
                    "vector",
                    lambda e, par=par: e.tensor_scalar(out=W2S[par][:], in0=QIA[:],
                                                       scalar1=1.0, scalar2=-1.0,
                                                       op0=OP.min, op1=OP.max), inc="dve")
                w2s_tok[jc] = qdve_tok[(jc, 1)]
                w2wr_tok[jc] = P.op("gpsimd", dma(w2q_scr[jc], W2S[par][:]),
                                    wait=[qdve_tok[(jc, 1)]], inc="gs", n=16)
                if 2 <= jc < 14:
                    emit_w3_step()
            else:
                wL = [ff1_tok.get(jc - 2)]
                if jc < 2:
                    wL.extend([prev_tile.get("ff143"), prev_tile.get("w1wr"),
                               prev_tile.get("w2wr")])
                P.op("sync", dma(W1S[par][:], w1q_scr[jc]),
                     wait=wL, inc=f"sL{par}", n=16)
                tl2 = P.op("sync", dma(W2S[par][:], w2q_scr[jc]), inc=f"sL{par}", n=16)
                w1s_tok[jc] = tl2
                w2s_tok[jc] = tl2
            for kc in range(KC):
                for h2 in range(2):
                    w1w = [w1s_tok[jc], xl2t_all,
                           sig_tok.get(jc - 2), hid_tok.get(jc - 2)]
                    if jc == 0 and kc == 0 and h2 == 0:
                        w1w.append(prev_tile.get("evac15"))
                    P.op("tensor",
                         lambda e, kc=kc, h2=h2, par=par: e.matmul(
                             X1PS[par][:, h2 * 512:(h2 + 1) * 512],
                             W1S[par][:, kc * 128:(kc + 1) * 128],
                             XLN2T[kc][:, h2 * 512:(h2 + 1) * 512],
                             start=(kc == 0), stop=(kc == KC - 1)),
                         wait=w1w, inc="pe")
                    ff1_tok[jc] = P.op(
                        "tensor",
                        lambda e, kc=kc, h2=h2, par=par: e.matmul(
                            X2PS[par][:, h2 * 512:(h2 + 1) * 512],
                            W2S[par][:, kc * 128:(kc + 1) * 128],
                            XLN2T[kc][:, h2 * 512:(h2 + 1) * 512],
                            start=(kc == 0), stop=(kc == KC - 1)),
                        wait=[w2s_tok[jc]], inc="pe")
            sig_tok[jc] = P.op("scalar",
                               lambda e, par=par: e.activation(out=SIG[par][:],
                                                               in_=X1PS[par][:],
                                                               func=AF.Sigmoid),
                               wait=[ff1_tok[jc]], inc="act")
            if jc > 0:
                emit_hid(jc - 1)
        emit_hid(JC - 1)
        if DBG and tile == 0:
            for jc in range(JC):
                P.op("gpsimd", dma(hid_dbg[jc], HID[jc][:]), wait=[hid_tok[jc]],
                     inc="gs", n=16)
        f1_ps.close()

        if tile == 0:
            w3pq["qfa0free"] = qdve_tok[(42, 0)]
        f2_ps = contextlib.ExitStack()
        OPS = [f2_ps.enter_context(nc.psum_tensor(f"OPS{tile}{i}", [128, 1024], F32))
               for i in range(4)]
        w3s_tok = {}
        omm_tok = {}
        xps_tok = {}
        tout_tok = {}

        def emit_out(fc, omm_tok=omm_tok, xps_tok=xps_tok, tout_tok=tout_tok,
                     OPS=OPS, tile=tile, tc0=tc0):
            par = fc % 2
            wout = [omm_tok[fc], xps_tok[fc], out_tok.get((tile, fc - 2))]
            if tile == 0 and fc < 2:
                wout.append(w3pq["mult"][10 + fc])
            tout_tok[fc] = P.op("vector",
                                lambda e, par=par, fc=fc: e.tensor_tensor(
                                    out=OUT_SB[par][:], in0=OPS[fc % 4][:],
                                    in1=QFA[1 + par][:, 0:1024], op=OP.add),
                                wait=wout, inc="dve")
            out_tok[(tile, fc)] = P.op("gpsimd",
                                       dma(y[fc * 128:(fc + 1) * 128, tc0:tc0 + 1024],
                                           OUT_SB[par][:]),
                                       wait=[tout_tok[fc]], inc="gs", n=16)

        for fc in range(16):
            par = fc % 2
            if tile == 0:
                w3s_tok[fc] = w3pq["asm"][fc]
            else:
                w3 = [omm_tok.get(fc - 2), w3pq["st"][fc]]
                if fc < 2:
                    w3.append(prev_tile.get("omm14" if fc == 0 else "omm15"))
                w3s_tok[fc] = P.op("sync", dma(W3S[par][:], w3q_scr[fc]),
                                   wait=w3, inc=f"s3{par}", n=16)
            wxps = [out_tok.get((tile, fc - 2))]
            if tile == 0 and fc < 2:
                wxps.append(qdve_tok.get((JC - 1, 1 - par)))
            xps_tok[fc] = P.op("sync",
                               dma(QFA[1 + par][:, 0:1024], xp_scr[fc][:, tc0:tc0 + 1024]),
                               wait=wxps, inc=f"sp{par}", n=16)
            for jc in range(JC):
                for h2 in range(2):
                    wo = [w3s_tok[fc], hid_tok[JC - 1],
                          out_tok.get((tile, fc - 4))]
                    if fc == 0 and jc == 0:
                        wo.append(sig_tok[JC - 1])
                    omm_tok[fc] = P.op(
                        "tensor",
                        lambda e, jc=jc, h2=h2, par=par, fc=fc: e.matmul(
                            OPS[fc % 4][:, h2 * 512:(h2 + 1) * 512],
                            W3S[par][:, jc * 128:(jc + 1) * 128],
                            HID[jc][:, h2 * 512:(h2 + 1) * 512],
                            start=(jc == 0), stop=(jc == JC - 1)),
                        wait=wo, inc="pe")
            if fc > 0:
                emit_out(fc - 1)
            if tile == 0 and fc < 14:
                w3pq["ommref"][fc] = omm_tok[fc]
                for _ in range(6):
                    emit_w3_step()
        emit_out(15)
        prev_tile = {"omm14": omm_tok.get(14), "omm15": omm_tok.get(15),
                     "w1wr": w1wr_tok.get(JC - 1), "w2wr": w2wr_tok.get(JC - 1),
                     "ff143": ff1_tok.get(JC - 1),
                     "evac15": tout_tok.get(15)}
        f2_ps.close()

    f_es.close()
    es.close()

    sem_es = contextlib.ExitStack()
    sems = {s: sem_es.enter_context(nc.semaphore(s)) for s in sorted(P.cnt)}
    with sem_es:
        with nc.Block() as block:
            def runner(name):
                def f(e):
                    for wl, fn, inc, n in P.ops[name]:
                        for (s, v) in wl:
                            e.wait_ge(sems[s], v)
                        if fn is not None:
                            ins = fn(e)
                            if inc is not None:
                                ins.then_inc(sems[inc], n)
                    if name in ("gpsimd", "sync"):
                        for s, v in P.cnt.items():
                            e.wait_ge(sems[s], v)
                return f
            block.sync(runner("sync"))
            block.gpsimd(runner("gpsimd"))
            block.tensor(runner("tensor"))
            block.scalar(runner("scalar"))
            block.vector(runner("vector"))
    return nc


def host_inputs(L0_attn_x, L0_attn_w_qkv, L0_attn_w_proj, L0_attn_b_proj,
                L0_ffn_w1, L0_ffn_w2, L0_ffn_w3):
    wq = np.asarray(L0_attn_w_qkv, np.float32)
    WqT = wq[0:2048].T
    K = wq[2048:4096].T
    V = wq[4096:6144].T
    wq_blk = np.ascontiguousarray(
        WqT.reshape(16, 128, 16, 128).transpose(2, 1, 0, 3)).reshape(16, 128, 2048)
    slabs = []
    for gg in range(8):
        g0, g1 = 2 * gg, 2 * gg + 1
        slab = np.concatenate([K[:, g0 * 128:(g0 + 1) * 128], K[:, g1 * 128:(g1 + 1) * 128],
                               V[:, g0 * 128:(g0 + 1) * 128], V[:, g1 * 128:(g1 + 1) * 128]],
                              axis=1)
        slabs.append(np.ascontiguousarray(slab).reshape(16, 128, 512))
    wkv_blk = np.ascontiguousarray(np.stack(slabs))
    WpT = np.asarray(L0_attn_w_proj, np.float32).T
    wpj_blk = np.ascontiguousarray(
        WpT.reshape(16, 128, 16, 128).transpose(2, 1, 0, 3)).reshape(16, 128, 2048)
    W1T = np.asarray(L0_ffn_w1, np.float32).T
    w1_blk = np.ascontiguousarray(
        W1T.reshape(16, 128, 44, 128).transpose(2, 1, 0, 3)).reshape(44, 128, 2048)
    W2T = np.asarray(L0_ffn_w2, np.float32).T
    w2_blk = np.ascontiguousarray(
        W2T.reshape(16, 128, 44, 128).transpose(2, 1, 0, 3)).reshape(44, 128, 2048)
    W3T = np.asarray(L0_ffn_w3, np.float32).T
    w3_blk = np.ascontiguousarray(
        W3T.reshape(44, 128, 16, 128).transpose(2, 1, 0, 3)).reshape(16, 128, 5632)
    bcol = np.ascontiguousarray(np.asarray(L0_attn_b_proj, np.float32).reshape(16, 128).T)
    ones_stat = np.full((128, 128), 1.0 / 2048.0, np.float32)
    ones_col = np.ones((128, 1), np.float32)
    onesrow = np.ones((1, 128), np.float32)
    mask = np.zeros((128, 128), np.float32)
    for h in range(4):
        mask[h * 32:(h + 1) * 32, h * 32:(h + 1) * 32] = 1.0
    epss = np.zeros((128, 2), np.float32)
    epss[:, 0] = 1e-5
    epss[:, 1] = 1e-15
    shared = dict(wq_blk=wq_blk, wkv_blk=wkv_blk, wpj_blk=wpj_blk, w1_blk=w1_blk,
                  w2_blk=w2_blk, w3_blk=w3_blk, bcol=bcol, ones_stat=ones_stat,
                  ones_col=ones_col, onesrow=onesrow, mask_bd=mask, epss=epss)
    x = np.asarray(L0_attn_x, np.float32)
    in_maps = []
    for b in range(8):
        m = dict(shared)
        m["xfm"] = np.ascontiguousarray(x[b].T)
        in_maps.append(m)
    return in_maps


def kernel(L0_attn_x, L0_attn_w_qkv, L0_attn_w_proj, L0_attn_b_proj,
           L0_ffn_w1, L0_ffn_w2, L0_ffn_w3):
    in_maps = host_inputs(L0_attn_x, L0_attn_w_qkv, L0_attn_w_proj, L0_attn_b_proj,
                          L0_ffn_w1, L0_ffn_w2, L0_ffn_w3)
    nc = build()
    res = run_bass_kernel_spmd(nc, in_maps, core_ids=list(range(8)))
    out = np.empty((8, T, D), np.float32)
    for b in range(8):
        out[b] = np.asarray(res.results[b]["y"]).T
    return out


# version 20
# speedup vs baseline: 1.0497x; 1.0497x over previous
import contextlib
import numpy as np
import concourse.bass as bass
import concourse.mybir as mybir
from concourse.bass_utils import run_bass_kernel_spmd
from concourse.alu_op_type import AluOpType as OP

F32 = mybir.dt.float32
BF16 = mybir.dt.bfloat16
I32 = mybir.dt.int32
AF = mybir.ActivationFunctionType
AX = mybir.AxisListType.X

ENGINES = ("sync", "gpsimd", "tensor", "scalar", "vector")

D = 2048
T = 2048
KC = 16
H = 5632
JC = 44
NW = float(H * D)
DBG = False


class Prog:
    def __init__(self):
        self.ops = {e: [] for e in ENGINES}
        self.cnt = {}
        self.lastw = {e: {} for e in ENGINES}

    def op(self, eng, fn, wait=(), inc=None, n=1):
        wl = []
        for t in wait:
            if t is None:
                continue
            s, v = t
            if self.lastw[eng].get(s, 0) < v:
                wl.append((s, v))
                self.lastw[eng][s] = v
        tok = None
        if inc is not None:
            self.cnt[inc] = self.cnt.get(inc, 0) + n
            tok = (inc, self.cnt[inc])
        self.ops[eng].append((wl, fn, inc, n))
        return tok

    def barrier(self):
        toks = list(self.cnt.items())
        for e in ENGINES:
            wl = []
            for s, v in toks:
                if self.lastw[e].get(s, 0) < v:
                    wl.append((s, v))
                    self.lastw[e][s] = v
            if wl:
                self.ops[e].append((wl, None, None, 0))


def dma(dst, src):
    return lambda e: e.dma_start(out=dst, in_=src)


def build():
    nc = bass.Bass("TRN2", target_bir_lowering=False)
    P = Prog()

    xfm = nc.dram_tensor("xfm", [D, T], F32, kind="ExternalInput").ap()
    wq_blk = nc.dram_tensor("wq_blk", [16, 128, 2048], F32, kind="ExternalInput").ap()
    wkv_blk = nc.dram_tensor("wkv_blk", [8, 16, 128, 512], F32, kind="ExternalInput").ap()
    wpj_blk = nc.dram_tensor("wpj_blk", [16, 128, 2048], F32, kind="ExternalInput").ap()
    w1_blk = nc.dram_tensor("w1_blk", [44, 128, 2048], F32, kind="ExternalInput").ap()
    w2_blk = nc.dram_tensor("w2_blk", [44, 128, 2048], F32, kind="ExternalInput").ap()
    w3_blk = nc.dram_tensor("w3_blk", [16, 128, 5632], F32, kind="ExternalInput").ap()
    bcol_in = nc.dram_tensor("bcol", [128, 16], F32, kind="ExternalInput").ap()
    ones_stat_in = nc.dram_tensor("ones_stat", [128, 128], F32, kind="ExternalInput").ap()
    ones_col_in = nc.dram_tensor("ones_col", [128, 1], F32, kind="ExternalInput").ap()
    onesrow_in = nc.dram_tensor("onesrow", [1, 128], F32, kind="ExternalInput").ap()
    mask_in = nc.dram_tensor("mask_bd", [128, 128], F32, kind="ExternalInput").ap()
    epss_in = nc.dram_tensor("epss", [128, 2], F32, kind="ExternalInput").ap()

    y = nc.dram_tensor("y", [D, T], F32, kind="ExternalOutput").ap()
    sk = dict(kind="ExternalOutput") if DBG else {}
    xp_scr = nc.dram_tensor("xp_scr", [16, 128, T], F32, **sk).ap()
    xl2_scr = nc.dram_tensor("xl2_scr", [16, 128, T], BF16, **sk).ap()
    w1q_scr = nc.dram_tensor("w1q_scr", [44, 128, 2048], BF16, **sk).ap()
    w2q_scr = nc.dram_tensor("w2q_scr", [44, 128, 2048], BF16, **sk).ap()
    w3q_scr = nc.dram_tensor("w3q_scr", [16, 128, 5632], BF16, **sk).ap()
    if DBG:
        xln_dbg = nc.dram_tensor("xln_dbg", [16, 128, T], BF16, kind="ExternalOutput").ap()
        vk_dbg = nc.dram_tensor("vk_dbg", [16, 128, 128], BF16, kind="ExternalOutput").ap()
        ks_dbg = nc.dram_tensor("ks_dbg", [16, 128, 128], BF16, kind="ExternalOutput").ap()
        attn_dbg = nc.dram_tensor("attn_dbg", [2, 16, 128, 1024], BF16,
                                  kind="ExternalOutput").ap()
        scl_dbg = nc.dram_tensor("scl_dbg", [128, 4], F32, kind="ExternalOutput").ap()
        hid_dbg = nc.dram_tensor("hid_dbg", [44, 128, 1024], BF16, kind="ExternalOutput").ap()

    es = contextlib.ExitStack()
    sb = lambda name, shape, dt: es.enter_context(nc.sbuf_tensor(name, shape, dt))

    MASK = sb("MASK", [128, 128], F32)
    ONES_ST = sb("ONES_ST", [128, 128], BF16)
    ONES_CB = sb("ONES_CB", [128, 1], BF16)
    ONES_CF = sb("ONES_CF", [128, 1], F32)
    ONESROW = sb("ONESROW", [1, 128], F32)
    GACC = sb("GACC", [128, 272], F32)
    SCLA = sb("SCLA", [1, 4], F32)
    SCLB = sb("SCLB", [1, 4], F32)
    S_COL = sb("S_COL", [128, 4], F32)
    EPS_SB = sb("EPS_SB", [128, 2], F32)

    ab_es = contextlib.ExitStack()
    sba = lambda name, shape, dt: ab_es.enter_context(nc.sbuf_tensor(name, shape, dt))
    XLN = [sba(f"XLN{i}", [128, T], BF16) for i in range(KC)]
    VKBLK = [sba(f"VKB{i}", [128, 128], BF16) for i in range(16)]
    KSBLK = [sba(f"KSB{i}", [128, 128], BF16) for i in range(16)]
    B_COL = sba("B_COL", [128, 16], F32)
    MEAN_SB = sba("MEAN_SB", [128, 1024], F32)
    M2_SB = sba("M2_SB", [128, 1024], F32)
    VAR_SB = sba("VAR_SB", [128, 1024], F32)
    STD_SB = sba("STD_SB", [128, 1024], F32)
    RSTD_SB = sba("RSTD_SB", [128, 1024], F32)
    TMPF = sba("TMPF", [128, 1024], F32)
    GSUM = sba("GSUM", [128, 4], F32)

    P.op("sync", dma(MASK[:], mask_in), inc="sc", n=16)
    P.op("sync", dma(B_COL[:], bcol_in), inc="sc", n=16)
    P.op("sync", dma(ONESROW[:], onesrow_in), inc="sc", n=16)
    P.op("sync", dma(ONES_CF[:], ones_col_in), inc="sc", n=16)
    t_cs = P.op("sync", dma(EPS_SB[:], epss_in), inc="sc", n=16)
    P.op("gpsimd", dma(ONES_ST[:], ones_stat_in), inc="gc", n=16)
    t_cg = P.op("gpsimd", dma(ONES_CB[:], ones_col_in), inc="gc", n=16)

    a_es = contextlib.ExitStack()
    sbA = lambda name, shape, dt: a_es.enter_context(nc.sbuf_tensor(name, shape, dt))
    XTS = [sbA(f"XTS{i}", [128, 512], F32) for i in range(KC)]
    XB = [sbA(f"XB{i}", [128, 512], BF16) for i in range(2)]
    XSQ = [sbA(f"XSQ{i}", [128, 512], BF16) for i in range(2)]
    WKV = [sbA(f"WKV{i}", [128, 16 * 512], BF16) for i in range(2)]
    WKVF = [sbA(f"WKVF{i}", [128, 2048], F32) for i in range(2)]
    GBUF = [sbA(f"GBUF{i}", [128, 1024], F32) for i in range(4)]
    KT = [sbA(f"KT{i}", [128, 256], BF16) for i in range(3)]
    VT = [sbA(f"VT{i}", [128, 256], BF16) for i in range(3)]
    KS_SB = sbA("KS_SB", [128, 16], F32)

    gsteps = []
    for jc in range(44):
        for c0 in (0, 1024):
            gsteps.append((w1_blk[jc][:, c0:c0 + 1024], 1024))
    for jc in range(44):
        for c0 in (0, 1024):
            gsteps.append((w2_blk[jc][:, c0:c0 + 1024], 1024))
    for fc in range(16):
        for c0 in range(0, 5632, 1024):
            w = min(1024, 5632 - c0)
            gsteps.append((w3_blk[fc][:, c0:c0 + w], w))
    n_gsteps = len(gsteps)
    g_state = {"i": 0, "acttok": {}}

    def emit_gamma_step():
        i = g_state["i"]
        if i >= n_gsteps:
            return
        g_state["i"] = i + 1
        src, w = gsteps[i]
        t_ld = P.op("sync", dma(GBUF[i % 4][:, :w], src),
                    wait=[g_state["acttok"].get(i - 4)], inc=f"sg{i % 4}", n=16)

        def f(e, i=i, w=w):
            return e.tensor_reduce(out=GACC[:, i:i + 1], in_=GBUF[i % 4][:, :w],
                                   axis=AX, op=OP.add, apply_absolute_value=True)
        g_state["acttok"][i] = P.op("vector", f, wait=[t_ld], inc="dve")

    kvmm_tok = {}
    wkv_tok = {}
    wkvf_cast = {}
    wkv_ld = {}

    def emit_wkv_load(gg, gi):
        i = gg * 4 + gi
        k0 = 4 * gi
        srcap = wkv_blk[gg, k0:k0 + 4].rearrange("k p f -> p k f")
        stg = WKVF[i % 2][:].rearrange("p (k f) -> p k f", k=4)
        wkv_ld[i] = P.op("gpsimd", dma(stg, srcap), wait=[wkvf_cast.get(i - 2)],
                         inc=f"skv{i % 2}", n=16)

    def emit_wkv_cast(gg, gi):
        i = gg * 4 + gi
        k0 = 4 * gi
        par = gg % 2
        wkvf_cast[i] = P.op("scalar",
                            lambda e, par=par, k0=k0, i=i: e.activation(
                                out=WKV[par][:, k0 * 512:(k0 + 4) * 512],
                                in_=WKVF[i % 2][:], func=AF.Copy),
                            wait=[wkv_ld[i], kvmm_tok.get((gg - 2, 15))], inc="act")
        if gi == 3:
            wkv_tok[gg] = wkvf_cast[i]

    emit_wkv_load(0, 0)
    emit_wkv_load(0, 1)

    a1_ps = contextlib.ExitStack()
    MEAN_PS = a1_ps.enter_context(nc.psum_tensor("MEAN_PS", [128, 512], F32))
    MSQ_PS = a1_ps.enter_context(nc.psum_tensor("MSQ_PS", [128, 512], F32))

    apply_tok = {}
    xln_tok = {}
    mm_stat_tok = {}
    psum_free_tok = {}
    for ts in range(4):
        c0 = ts * 512
        txs = {}
        for kc in range(KC):
            txs[kc] = P.op("sync", dma(XTS[kc][:], xfm[kc * 128:(kc + 1) * 128, c0:c0 + 512]),
                           wait=[apply_tok.get((ts - 1, kc))], inc="sx", n=16)
        cast_tok = {}
        for kc in range(KC):
            cast_tok[(kc, 0)] = P.op(
                "scalar",
                lambda e, kc=kc: e.activation(out=XB[kc % 2][:], in_=XTS[kc][:], func=AF.Copy),
                wait=[txs[kc], mm_stat_tok.get((ts, kc - 2, 0))], inc="act")
            cast_tok[(kc, 1)] = P.op(
                "scalar",
                lambda e, kc=kc: e.activation(out=XSQ[kc % 2][:], in_=XTS[kc][:],
                                              func=AF.Square),
                wait=[mm_stat_tok.get((ts, kc - 2, 1))], inc="act")
            war_ps = psum_free_tok.get(ts - 1) if kc == 0 else None
            mm_stat_tok[(ts, kc, 0)] = P.op(
                "tensor",
                lambda e, kc=kc: e.matmul(MEAN_PS[:], ONES_ST[:], XB[kc % 2][:],
                                          start=(kc == 0), stop=(kc == KC - 1)),
                wait=[cast_tok[(kc, 0)], t_cg, war_ps], inc="pe")
            mm_stat_tok[(ts, kc, 1)] = P.op(
                "tensor",
                lambda e, kc=kc: e.matmul(MSQ_PS[:], ONES_ST[:], XSQ[kc % 2][:],
                                          start=(kc == 0), stop=(kc == KC - 1)),
                wait=[cast_tok[(kc, 1)]], inc="pe")
        emit_wkv_cast(0, ts)
        if ts < 2:
            emit_wkv_load(0, ts + 2)
        else:
            emit_wkv_load(1, ts - 2)
        t1 = P.op("vector", lambda e: e.tensor_copy(out=MEAN_SB[:, :512], in_=MEAN_PS[:]),
                  wait=[mm_stat_tok[(ts, KC - 1, 0)]], inc="dve")
        t2 = P.op("scalar", lambda e: e.activation(out=M2_SB[:, :512], in_=MEAN_SB[:, :512],
                                                   func=AF.Square), wait=[t1], inc="act")
        t3 = P.op("vector", lambda e: e.tensor_tensor(out=VAR_SB[:, :512], in0=MSQ_PS[:],
                                                      in1=M2_SB[:, :512], op=OP.subtract),
                  wait=[mm_stat_tok[(ts, KC - 1, 1)], t2], inc="dve")
        psum_free_tok[ts] = t3
        t4 = P.op("scalar", lambda e: e.activation(out=STD_SB[:, :512], in_=VAR_SB[:, :512],
                                                   func=AF.Ln, bias=EPS_SB[:, 0:1]),
                  wait=[t3, t_cs], inc="act")
        t5 = P.op("scalar", lambda e: e.activation(out=RSTD_SB[:, :512], in_=STD_SB[:, :512],
                                                   func=AF.Exp, scale=-0.5),
                  wait=[t4], inc="act")
        for kc in range(KC):
            P.op("vector",
                 lambda e, kc=kc: e.tensor_tensor(out=TMPF[:, :512], in0=XTS[kc][:],
                                                  in1=MEAN_SB[:, :512], op=OP.subtract),
                 wait=[t5], inc="dve")
            tok = P.op("vector",
                       lambda e, kc=kc, c0=c0: e.tensor_tensor(
                           out=XLN[kc][:, c0:c0 + 512], in0=TMPF[:, :512],
                           in1=RSTD_SB[:, :512], op=OP.mult), inc="dve")
            apply_tok[(ts, kc)] = tok
            xln_tok[(ts, kc)] = tok
        for _ in range(4):
            emit_gamma_step()

    a1_ps.close()

    a2_ps = contextlib.ExitStack()
    KVPS = [a2_ps.enter_context(nc.psum_tensor(f"KVPS{i}", [128, 512], F32)) for i in range(3)]
    VKPS = a2_ps.enter_context(nc.psum_tensor("VKPS", [128, 2048], F32))
    KSPS = a2_ps.enter_context(nc.psum_tensor("KSPS", [128, 512], F32))

    def prev_r_use(gg, tt):
        if tt >= 3:
            return (gg, tt - 3)
        if gg == 0:
            return None
        return (gg - 1, 15 if tt == 0 else tt + 12)

    evac_tok = {}
    eins_tok = {}
    pend_eins = None

    def emit_eins(gg, tt, r):
        for g01 in range(2):
            g = 2 * gg + g01
            P.op("tensor",
                 lambda e, r=r, g01=g01, g=g, tt=tt: e.matmul(
                     VKPS[:, g * 128:(g + 1) * 128], KT[r][:, g01 * 128:(g01 + 1) * 128],
                     VT[r][:, g01 * 128:(g01 + 1) * 128],
                     start=(tt == 0 and g % 4 == 0), stop=(tt == 15)),
                 wait=[evac_tok[(gg, tt)], t_cg], inc="pe")
            eins_tok[(gg, tt)] = P.op(
                "tensor",
                lambda e, r=r, g01=g01, g=g, tt=tt: e.matmul(
                    KSPS[:, g:g + 1], KT[r][:, g01 * 128:(g01 + 1) * 128], ONES_CB[:],
                    start=(tt == 0 and g == 0), stop=(tt == 15)),
                inc="pe")

    for gg in range(8):
        par = gg % 2
        for tt in range(16):
            r = tt % 3
            if gg < 7:
                if tt == 0:
                    emit_wkv_load(gg + 1, 0)
                elif tt == 1:
                    emit_wkv_load(gg + 1, 1)
                elif tt == 4:
                    emit_wkv_cast(gg + 1, 0)
                elif tt == 5:
                    emit_wkv_load(gg + 1, 2)
                elif tt == 7:
                    emit_wkv_cast(gg + 1, 1)
                elif tt == 8:
                    emit_wkv_load(gg + 1, 3)
                elif tt == 10:
                    emit_wkv_cast(gg + 1, 2)
                elif tt == 13:
                    emit_wkv_cast(gg + 1, 3)
            for kc in range(KC):
                w = [xln_tok[(tt // 4, kc)], wkv_tok[gg]]
                if kc == 0:
                    w.append(evac_tok.get(prev_r_use(gg, tt)))
                    if gg == 0 and tt < 3:
                        w.append(psum_free_tok[3])
                kvmm_tok[(gg, tt)] = P.op(
                    "tensor",
                    lambda e, kc=kc, r=r, tt=tt, par=par: e.matmul(
                        KVPS[r][:], XLN[kc][:, tt * 128:(tt + 1) * 128],
                        WKV[par][:, kc * 512:(kc + 1) * 512],
                        start=(kc == 0), stop=(kc == KC - 1)),
                    wait=w, inc="pe")
            if pend_eins is not None:
                emit_eins(*pend_eins)
            wE = [kvmm_tok[(gg, tt)], eins_tok.get(prev_r_use(gg, tt))]
            P.op("scalar", lambda e, r=r: e.activation(out=KT[r][:], in_=KVPS[r][:, 0:256],
                                                       func=AF.Relu), wait=wE, inc="act")
            evac_tok[(gg, tt)] = P.op(
                "scalar", lambda e, r=r: e.activation(out=VT[r][:], in_=KVPS[r][:, 256:512],
                                                      func=AF.Copy), inc="act")
            pend_eins = (gg, tt, r)
            emit_gamma_step()
            if tt % 2 == 0:
                emit_gamma_step()
    emit_eins(*pend_eins)

    vk_done = eins_tok[(7, 15)]

    t_ks = P.op("vector", lambda e: e.tensor_copy(out=KS_SB[:], in_=KSPS[:, 0:16]),
                wait=[vk_done, t_cs], inc="dve")
    vkblk_tok = {}
    for g in range(16):
        P.op("vector",
             lambda e, g=g: e.tensor_tensor(out=VKBLK[g][:],
                                            in0=VKPS[:, g * 128:(g + 1) * 128],
                                            in1=MASK[:], op=OP.mult), inc="dve")
        vkblk_tok[g] = P.op("vector",
                            lambda e, g=g: e.tensor_scalar(out=KSBLK[g][:], in0=MASK[:],
                                                           scalar1=KS_SB[:, g:g + 1],
                                                           scalar2=None, op0=OP.mult),
                            inc="dve")
    if DBG:
        for g in range(16):
            P.op("gpsimd", dma(vk_dbg[g], VKBLK[g][:]), wait=[vkblk_tok[g]], inc="gs", n=16)
            P.op("gpsimd", dma(ks_dbg[g], KSBLK[g][:]), inc="gs", n=16)
        for kc in range(KC):
            P.op("gpsimd", dma(xln_dbg[kc], XLN[kc][:]), inc="gs", n=16)
    a2_ps.close()

    P.barrier()
    a_es.close()

    b_es = contextlib.ExitStack()
    sbB = lambda name, shape, dt: b_es.enter_context(nc.sbuf_tensor(name, shape, dt))
    ATTN = [sbB(f"ATTN{i}", [128, 1024], BF16) for i in range(16)]
    WQS = [sbB(f"WQS{i}", [128, 2048], BF16) for i in range(2)]
    WPS = [sbB(f"WPS{i}", [128, 2048], BF16) for i in range(2)]
    QSB = [sbB(f"QSB{i}", [128, 1024], BF16) for i in range(2)]
    DEN_SB = sbB("DEN_SB", [128, 1024], F32)
    RCP_SB = sbB("RCP_SB", [128, 1024], F32)
    XRES = [sbB(f"XRES{i}", [128, 1024], F32) for i in range(2)]
    XP_SB = [sbB(f"XP{i}", [128, 1024], F32) for i in range(2)]
    XB2 = [sbB(f"XB2_{i}", [128, 1024], BF16) for i in range(2)]
    XSQ2 = [sbB(f"XSQ2_{i}", [128, 1024], BF16) for i in range(2)]
    XL2 = [sbB(f"XL2_{i}", [128, 1024], BF16) for i in range(2)]
    WSTG = [sbB(f"WSTG{i}", [128, 2048], F32) for i in range(2)]

    xp_store_tok = {}
    gb_lastload = {}

    def emit_gamma_step_b(extra=(), pair=None):
        i = g_state["i"]
        if i >= n_gsteps:
            return
        g_state["i"] = i + 1
        src, w = gsteps[i]
        p2 = i % len(pair)
        stg = pair[p2]
        lw = [g_state["acttok"].get(i - len(pair))] + list(extra)
        t_ld = P.op("gpsimd", dma(stg[:, 0:w], src), wait=lw, inc=f"gb{p2}", n=16)
        gb_lastload[p2] = t_ld

        def f(e, i=i, w=w, stg=stg):
            return e.tensor_reduce(out=GACC[:, i:i + 1], in_=stg[:, 0:w],
                                   axis=AX, op=OP.add, apply_absolute_value=True)
        g_state["acttok"][i] = P.op("vector", f, wait=[t_ld], inc="dve")

    def emit_apply_chunk(actx):
        if actx is None:
            return
        kc = actx["kc"]
        if kc >= KC:
            return
        actx["kc"] = kc + 1
        th_, h0_, nL = actx["th"], actx["h0"], actx["nL"]
        s = kc % nL
        par = kc % 2
        P.op("vector",
             lambda e, actx=actx, s=s: e.tensor_tensor(out=TMPF[:], in0=actx["rslot"][s],
                                                       in1=MEAN_SB[:], op=OP.subtract),
             wait=[actx["xpb"][kc], actx["t5"], actx["xl2st"].get(kc - 2)], inc="dve")
        actx["xl2"][kc] = P.op("vector",
                               lambda e, par=par: e.tensor_tensor(out=XL2[par][:],
                                                                  in0=TMPF[:],
                                                                  in1=RSTD_SB[:],
                                                                  op=OP.mult),
                               inc="dve")
        if kc + nL < KC:
            s2 = (kc + nL) % nL
            t_l = P.op("gpsimd",
                       dma(actx["rslot"][s2], xp_scr[kc + nL][:, h0_:h0_ + 1024]),
                       wait=[xp_store_tok[(th_, kc + nL)], actx["xl2"][kc]],
                       inc=f"bx{th_}{s2}", n=16)
            actx["xpb"][kc + nL] = t_l
            if s2 == 4:
                actx["var_last_ld"] = t_l
        actx["xl2st"][kc] = P.op("gpsimd",
                                 dma(xl2_scr[kc][:, h0_:h0_ + 1024], XL2[par][:]),
                                 wait=[actx["xl2"][kc]], inc=f"gx{par}", n=16)

    prev_th = {}
    prev_actx = None
    for th in range(2):
        h0 = th * 1024
        pa = prev_actx
        gb_pair = (TMPF, MEAN_SB, M2_SB, STD_SB)
        gb_cnt = [0]
        bq_ps = contextlib.ExitStack()
        QPS = [bq_ps.enter_context(nc.psum_tensor(f"QPS{th}{i}", [128, 1024], F32))
               for i in range(2)]
        NUMPS = bq_ps.enter_context(nc.psum_tensor(f"NUMPS{th}", [128, 1024], F32))
        DENPS = bq_ps.enter_context(nc.psum_tensor(f"DENPS{th}", [128, 1024], F32))
        wqs_tok = {}
        qmm_tok = {}
        relu_tok = {}
        nd_tok = {}
        den_tok = {}
        rcp_tok = {}
        attn_tok = {}

        def emit_numden(qc, attn_tok=attn_tok, relu_tok=relu_tok, nd_tok=nd_tok,
                        den_tok=den_tok, rcp_tok=rcp_tok, NUMPS=NUMPS, DENPS=DENPS):
            par = qc % 2
            for h2 in range(2):
                wn = [relu_tok[qc], attn_tok.get(qc - 1), vkblk_tok[15]]
                if qc == 0:
                    wn.append(prev_th.get("stats_free"))
                P.op("tensor",
                     lambda e, qc=qc, h2=h2, par=par: e.matmul(
                         NUMPS[:, h2 * 512:(h2 + 1) * 512], VKBLK[qc][:],
                         QSB[par][:, h2 * 512:(h2 + 1) * 512], start=True, stop=True),
                     wait=wn, inc="pe")
                nd_tok[qc] = P.op(
                    "tensor",
                    lambda e, qc=qc, h2=h2, par=par: e.matmul(
                        DENPS[:, h2 * 512:(h2 + 1) * 512], KSBLK[qc][:],
                        QSB[par][:, h2 * 512:(h2 + 1) * 512], start=True, stop=True),
                    wait=[den_tok.get(qc - 1)], inc="pe")
            den_tok[qc] = P.op("scalar",
                               lambda e: e.activation(out=DEN_SB[:], in_=DENPS[:],
                                                      func=AF.Ln, bias=EPS_SB[:, 1:2]),
                               wait=[nd_tok[qc], rcp_tok.get(qc - 1), t_cs], inc="act")
            rcp_tok[qc] = P.op("scalar",
                               lambda e: e.activation(out=RCP_SB[:], in_=DEN_SB[:],
                                                      func=AF.Exp, scale=-1.0),
                               wait=[attn_tok.get(qc - 1)], inc="act")
            attn_tok[qc] = P.op("vector",
                                lambda e, qc=qc: e.tensor_tensor(out=ATTN[qc][:], in0=NUMPS[:],
                                                                 in1=RCP_SB[:], op=OP.mult),
                                wait=[rcp_tok[qc]], inc="dve")

        def emit_wq_load(qc, wqs_tok=wqs_tok, qmm_tok=qmm_tok, wstg_prev=None,
                         twq_pre=None):
            par = qc % 2
            if twq_pre is not None:
                twq = twq_pre
            else:
                twq = P.op("sync", dma(WSTG[par][:], wq_blk[qc]),
                           wait=[wqs_tok.get(qc - 2), wstg_prev], inc=f"sw{par}", n=16)
            wqs_tok[qc] = P.op("scalar",
                               lambda e, par=par: e.activation(out=WQS[par][:],
                                                               in_=WSTG[par][:],
                                                               func=AF.Copy),
                               wait=[twq, qmm_tok.get(qc - 2),
                                     prev_th.get("qps_free")], inc="act")

        emit_wq_load(0, wstg_prev=prev_th.get("wstg0"),
                     twq_pre=prev_th.get("pre_wq0"))
        emit_wq_load(1, wstg_prev=prev_th.get("wstg1"),
                     twq_pre=prev_th.get("pre_wq1"))
        for qc in range(16):
            par = qc % 2
            for kc in range(KC):
                for h2 in range(2):
                    wq = [wqs_tok[qc], relu_tok.get(qc - 2)]
                    if qc == 0 and kc == 0:
                        wq.append(prev_th.get("pps_free"))
                        wq.append(prev_th.get("stats_free"))
                    qmm_tok[qc] = P.op(
                        "tensor",
                        lambda e, kc=kc, h2=h2, par=par, h0=h0: e.matmul(
                            QPS[par][:, h2 * 512:(h2 + 1) * 512],
                            WQS[par][:, kc * 128:(kc + 1) * 128],
                            XLN[kc][:, h0 + h2 * 512:h0 + (h2 + 1) * 512],
                            start=(kc == 0), stop=(kc == KC - 1)),
                        wait=wq, inc="pe")
            if qc + 1 < 16:
                emit_wq_load(qc + 1)
            if qc > 0:
                emit_numden(qc - 1)
            emit_apply_chunk(pa)
            emit_apply_chunk(pa)
            if th == 0:
                for _g in range(4):
                    emit_gamma_step_b((), gb_pair)
            relu_tok[qc] = P.op("scalar",
                                lambda e, par=par: e.activation(out=QSB[par][:],
                                                                in_=QPS[par][:], func=AF.Relu),
                                wait=[qmm_tok[qc], nd_tok.get(qc - 2)], inc="act")
        emit_numden(15)
        if DBG:
            for qc in range(16):
                P.op("gpsimd", dma(attn_dbg[th, qc], ATTN[qc][:]), wait=[attn_tok[qc]],
                     inc="gs", n=16)
        bq_ps.close()

        bp_ps = contextlib.ExitStack()
        PPS = [bp_ps.enter_context(nc.psum_tensor(f"PPS{th}{i}", [128, 1024], F32))
               for i in range(2)]
        MEAN2_PS = bp_ps.enter_context(nc.psum_tensor(f"MEAN2_{th}", [128, 1024], F32))
        MSQ2_PS = bp_ps.enter_context(nc.psum_tensor(f"MSQ2_{th}", [128, 1024], F32))
        wps_tok = {}
        pmm_tok = {}
        xres_tok = {}
        xp_tok = {}
        cast2_tok = {}
        smm_tok = {}

        def emit_stats(oc, cast2_tok=cast2_tok, smm_tok=smm_tok,
                       MEAN2_PS=MEAN2_PS, MSQ2_PS=MSQ2_PS,
                       attn_tok=attn_tok, den_tok=den_tok, rcp_tok=rcp_tok):
            par = oc % 2
            for h2 in range(2):
                ws = [cast2_tok[(oc, 0)]]
                if oc == 0:
                    ws.extend([attn_tok[15], den_tok[15], rcp_tok[15]])
                smm_tok[(oc, 0)] = P.op(
                    "tensor",
                    lambda e, h2=h2, oc=oc, par=par: e.matmul(
                        MEAN2_PS[:, h2 * 512:(h2 + 1) * 512], ONES_ST[:],
                        XB2[par][:, h2 * 512:(h2 + 1) * 512],
                        start=(oc == 0), stop=(oc == 15)),
                    wait=ws, inc="pe")
                smm_tok[(oc, 1)] = P.op(
                    "tensor",
                    lambda e, h2=h2, oc=oc, par=par: e.matmul(
                        MSQ2_PS[:, h2 * 512:(h2 + 1) * 512], ONES_ST[:],
                        XSQ2[par][:, h2 * 512:(h2 + 1) * 512],
                        start=(oc == 0), stop=(oc == 15)),
                    wait=[cast2_tok[(oc, 1)]], inc="pe")

        def emit_wp_load(oc, wps_tok=wps_tok, pmm_tok=pmm_tok, wqs_tok=wqs_tok):
            par = oc % 2
            twp = P.op("sync", dma(WSTG[par][:], wpj_blk[oc]),
                       wait=[wps_tok.get(oc - 2), wqs_tok[14 + par]], inc=f"sw{par}", n=16)
            wps_tok[oc] = P.op("scalar",
                               lambda e, par=par: e.activation(out=WPS[par][:],
                                                               in_=WSTG[par][:],
                                                               func=AF.Copy),
                               wait=[twp, pmm_tok.get(oc - 2)], inc="act")

        emit_wp_load(0)
        emit_wp_load(1)
        for oc in range(16):
            par = oc % 2
            xres_tok[oc] = P.op("sync",
                                dma(XRES[par][:], xfm[oc * 128:(oc + 1) * 128, h0:h0 + 1024]),
                                wait=[xp_tok.get(oc - 2),
                                      pa["xl2"].get(12 + oc)
                                      if (pa is not None and oc < 2) else None],
                                inc=f"sxr{par}", n=16)
            for ac in range(16):
                for h2 in range(2):
                    wp = [wps_tok[oc], attn_tok[15], xp_tok.get(oc - 2)]
                    if oc == 0 and ac == 0:
                        wp.extend([relu_tok[15], den_tok[15]])
                    pmm_tok[oc] = P.op(
                        "tensor",
                        lambda e, ac=ac, h2=h2, par=par: e.matmul(
                            PPS[par][:, h2 * 512:(h2 + 1) * 512],
                            WPS[par][:, ac * 128:(ac + 1) * 128],
                            ATTN[ac][:, h2 * 512:(h2 + 1) * 512],
                            start=(ac == 0), stop=(ac == 15)),
                        wait=wp, inc="pe")
            if oc + 1 < 16:
                emit_wp_load(oc + 1)
            if oc > 0:
                emit_stats(oc - 1)
            xp_tok[oc] = P.op("vector",
                              lambda e, oc=oc, par=par: e.scalar_tensor_tensor(
                                  out=XP_SB[par][:], in0=PPS[par][:],
                                  scalar=B_COL[:, oc:oc + 1], in1=XRES[par][:],
                                  op0=OP.add, op1=OP.add),
                              wait=[pmm_tok[oc], xres_tok[oc], t_cs,
                                    xp_store_tok.get((th, oc - 2)),
                                    cast2_tok.get((oc - 2, 1))], inc="dve")
            xp_store_tok[(th, oc)] = P.op("gpsimd",
                                          dma(xp_scr[oc][:, h0:h0 + 1024], XP_SB[par][:]),
                                          wait=[xp_tok[oc]], inc="gs", n=16)
            cast2_tok[(oc, 0)] = P.op("scalar",
                                      lambda e, par=par: e.activation(
                                          out=XB2[par][:], in_=XP_SB[par][:], func=AF.Copy),
                                      wait=[xp_tok[oc], smm_tok.get((oc - 2, 0))], inc="act")
            cast2_tok[(oc, 1)] = P.op("scalar",
                                      lambda e, par=par: e.activation(
                                          out=XSQ2[par][:], in_=XP_SB[par][:], func=AF.Square),
                                      wait=[smm_tok.get((oc - 2, 1))], inc="act")
        emit_stats(15)
        t1 = P.op("vector", lambda e: e.tensor_copy(out=MEAN_SB[:], in_=MEAN2_PS[:]),
                  wait=[smm_tok[(15, 0)], gb_lastload.get(1)],
                  inc="dve")
        t2 = P.op("scalar", lambda e: e.activation(out=M2_SB[:], in_=MEAN_SB[:],
                                                   func=AF.Square),
                  wait=[t1, gb_lastload.get(2)], inc="act")
        t3 = P.op("vector", lambda e: e.tensor_tensor(out=VAR_SB[:], in0=MSQ2_PS[:],
                                                      in1=M2_SB[:], op=OP.subtract),
                  wait=[smm_tok[(15, 1)], t2,
                        pa.get("var_last_ld") if pa is not None else None], inc="dve")
        t4 = P.op("scalar", lambda e: e.activation(out=STD_SB[:], in_=VAR_SB[:], func=AF.Ln,
                                                   bias=EPS_SB[:, 0:1]),
                  wait=[t3, gb_lastload.get(3)], inc="act")
        t5 = P.op("scalar", lambda e: e.activation(out=RSTD_SB[:], in_=STD_SB[:],
                                                   func=AF.Exp, scale=-0.5),
                  wait=[t4, pa["xl2"].get(15) if pa is not None else None], inc="act")
        pre_wq = {}
        if th == 0:
            for q in range(2):
                pre_wq[q] = P.op("sync", dma(WSTG[q][:], wq_blk[q]),
                                 wait=[wps_tok[14 + q]], inc=f"sw{q}", n=16)
        if th == 0:
            rslot = [XP_SB[0][:], XP_SB[1][:], XRES[0][:], XRES[1][:], VAR_SB[:]]
            rfree = [cast2_tok[(14, 1)], cast2_tok[(15, 1)], xp_tok[15], xp_tok[15], t5]
        else:
            rslot = [XP_SB[0][:], XP_SB[1][:], XRES[0][:], XRES[1][:], VAR_SB[:],
                     DEN_SB[:], RCP_SB[:]]
            rfree = [cast2_tok[(14, 1)], cast2_tok[(15, 1)], xp_tok[15], xp_tok[15], t5,
                     attn_tok[15], attn_tok[15]]
        rfree2 = [xp_store_tok[(th, 14)], xp_store_tok[(th, 15)]]
        actx = {"th": th, "h0": h0, "t5": t5, "kc": 0, "nL": len(rslot),
                "rslot": rslot, "xpb": {}, "xl2": {}, "xl2st": {}}
        if pa is not None:
            actx["xl2st"][-2] = pa["xl2st"].get(14)
            actx["xl2st"][-1] = pa["xl2st"].get(15)
        for s in range(actx["nL"]):
            actx["xpb"][s] = P.op("gpsimd", dma(rslot[s], xp_scr[s][:, h0:h0 + 1024]),
                                  wait=[xp_store_tok[(th, s)], rfree[s],
                                        rfree2[s] if s < 2 else None],
                                  inc=f"bx{th}{s}", n=16)
            if s == 4:
                actx["var_last_ld"] = actx["xpb"][s]
        prev_actx = actx
        prev_th = {
            "pps_free": xp_tok[15],
            "stats_free": t3,
            "qps_free": pmm_tok[15],
            "wstg0": wps_tok[14],
            "wstg1": wps_tok[15],
            "pre_wq0": pre_wq.get(0),
            "pre_wq1": pre_wq.get(1),
            "t3": t3,
            "t5": t5,
        }
        bp_ps.close()

    assert g_state["i"] == n_gsteps

    gf_ps = contextlib.ExitStack()
    GPS = gf_ps.enter_context(nc.psum_tensor("GPS", [1, 512], F32))
    BPS = gf_ps.enter_context(nc.psum_tensor("BPS", [128, 512], F32))
    g_last = g_state["acttok"][n_gsteps - 1]
    rtoks = []
    for m, (c0, c1) in enumerate(((0, 88), (88, 176), (176, 272))):
        rtoks.append(P.op("vector",
                          lambda e, m=m, c0=c0, c1=c1: e.tensor_reduce(
                              out=GSUM[:, m:m + 1], in_=GACC[:, c0:c1], axis=AX, op=OP.add),
                          wait=[g_last], inc="dve"))
    gmm = []
    for m in range(3):
        gmm.append(P.op("tensor",
                        lambda e, m=m: e.matmul(GPS[:, m:m + 1], GSUM[:, m:m + 1], ONES_CF[:],
                                                start=True, stop=True),
                        wait=[rtoks[m], t_cs, prev_th["pps_free"],
                              prev_th["stats_free"]], inc="pe"))
    tg1 = P.op("vector", lambda e: e.tensor_scalar(out=SCLA[:, 0:3], in0=GPS[0:1, 0:3],
                                                   scalar1=1.0 / NW, scalar2=1e-8,
                                                   op0=OP.mult, op1=OP.add),
               wait=[gmm[2]], inc="dve")
    tg2 = P.op("vector", lambda e: e.reciprocal(out=SCLB[:, 0:3], in_=SCLA[:, 0:3]),
               wait=[tg1], inc="dve")
    tg3 = P.op("tensor", lambda e: e.matmul(BPS[:, 0:3], ONESROW[:], SCLB[:, 0:3],
                                            start=True, stop=True),
               wait=[tg2], inc="pe")
    t_scol = P.op("vector", lambda e: e.tensor_copy(out=S_COL[:, 0:3], in_=BPS[:, 0:3]),
                  wait=[tg3], inc="dve")
    gf_ps.close()

    while prev_actx["kc"] < KC:
        emit_apply_chunk(prev_actx)

    P.barrier()
    b_es.close()
    ab_es.close()

    f_es = contextlib.ExitStack()
    sbF = lambda name, shape, dt: f_es.enter_context(nc.sbuf_tensor(name, shape, dt))

    HID = [sbF(f"HID{i}", [128, 1024], BF16) for i in range(JC)]
    XLN2T = [sbF(f"XL2T{i}", [128, 1024], BF16) for i in range(KC)]
    W1S = [sbF(f"W1S{i}", [128, 2048], BF16) for i in range(2)]
    W2S = [sbF(f"W2S{i}", [128, 2048], BF16) for i in range(2)]
    SIG = [sbF(f"SIG{i}", [128, 1024], BF16) for i in range(2)]
    QFA = [sbF(f"QFA{i}", [128, 2048], F32) for i in range(3)]
    QIA = sbF("QIA", [128, 2048], I32)
    W3S = [sbF(f"W3S{i}", [128, 5632], BF16) for i in range(2)]
    OUT_SB = [sbF(f"OUT{i}", [128, 1024], F32) for i in range(2)]

    w3_strips = []
    for fc in range(16):
        for c0 in range(0, 5632, 1024):
            w3_strips.append((fc, c0, min(1024, 5632 - c0)))
    w3pq = {"i": 0, "mm": {}, "mult": {}, "asm": {}, "st": {}, "ommref": {},
            "qfa0free": None}

    def emit_w3_step():
        s = w3pq["i"]
        if s >= len(w3_strips):
            return
        w3pq["i"] = s + 1
        fc, c0, w = w3_strips[s]
        p2 = s % 2
        if s < 12:
            dst = OUT_SB[p2][:, 0:w]
        else:
            dst = QFA[0][:, p2 * 1024:p2 * 1024 + w]
        lw = [w3pq["mm"].get(s - 2)]
        if s in (12, 13):
            lw.append(w3pq["qfa0free"])
        t_ld = P.op("sync", dma(dst, w3_blk[fc][:, c0:c0 + w]), wait=lw,
                    inc=f"q3{p2}", n=16)
        mw = [t_ld, t_scol]
        if c0 == 0 and fc >= 2:
            mw.extend([w3pq["ommref"][fc - 2], w3pq["st"][fc - 2]])
        w3pq["mult"][s] = P.op(
            "vector",
            lambda e, dst=dst, w=w: e.tensor_scalar(
                out=QIA[:, 0:w], in0=dst, scalar1=S_COL[:, 2:3], scalar2=None,
                op0=OP.mult),
            wait=mw, inc="dve")
        w3pq["mm"][s] = P.op(
            "vector",
            lambda e, fc=fc, c0=c0, w=w: e.tensor_scalar(
                out=W3S[fc % 2][:, c0:c0 + w], in0=QIA[:, 0:w],
                scalar1=1.0, scalar2=-1.0, op0=OP.min, op1=OP.max),
            inc="dve")
        if c0 == 5120:
            w3pq["asm"][fc] = w3pq["mm"][s]
            w3pq["st"][fc] = P.op("gpsimd", dma(w3q_scr[fc], W3S[fc % 2][:]),
                                  wait=[w3pq["mm"][s]], inc="g3", n=16)

    out_tok = {}
    prev_tile = {}
    for tile in range(2):
        tc0 = tile * 1024
        xl2t_tok = {}
        for kc in range(KC):
            xl2t_tok[kc] = P.op("sync", dma(XLN2T[kc][:], xl2_scr[kc][:, tc0:tc0 + 1024]),
                                wait=[prev_tile.get("ff143")], inc="sf", n=16)
        f1_ps = contextlib.ExitStack()
        X1PS = [f1_ps.enter_context(nc.psum_tensor(f"X1PS{tile}{i}", [128, 1024], F32))
                for i in range(2)]
        X2PS = [f1_ps.enter_context(nc.psum_tensor(f"X2PS{tile}{i}", [128, 1024], F32))
                for i in range(2)]
        w1s_tok = {}
        w2s_tok = {}
        w1wr_tok = {}
        w2wr_tok = {}
        ff1_tok = {}
        sig_tok = {}
        hid_tok = {}
        qdve_tok = {}

        def emit_hid(jc, sig_tok=sig_tok, hid_tok=hid_tok, X2PS=X2PS):
            par = jc % 2
            hid_tok[jc] = P.op("vector",
                               lambda e, jc=jc, par=par: e.tensor_tensor(
                                   out=HID[jc][:], in0=SIG[par][:], in1=X2PS[par][:],
                                   op=OP.mult),
                               wait=[sig_tok[jc]], inc="dve")

        for jc in range(JC):
            par = jc % 2
            if tile == 0:
                q1b = par * 2
                tq1 = P.op("sync", dma(QFA[q1b][:], w1_blk[jc]),
                           wait=[qdve_tok.get((jc - 2, 0))], inc=f"qa{par}", n=16)
                tq2 = P.op("sync", dma(QFA[1][:], w2_blk[jc]),
                           wait=[qdve_tok.get((jc - 1, 1))], inc="qb", n=16)
                P.op("vector",
                     lambda e, q1b=q1b: e.tensor_scalar(out=QIA[:], in0=QFA[q1b][:],
                                                        scalar1=S_COL[:, 0:1], scalar2=None,
                                                        op0=OP.mult),
                     wait=[tq1, t_scol, ff1_tok.get(jc - 2), w1wr_tok.get(jc - 2)],
                     inc="dve")
                qdve_tok[(jc, 0)] = P.op(
                    "vector",
                    lambda e, par=par: e.tensor_scalar(out=W1S[par][:], in0=QIA[:],
                                                       scalar1=1.0, scalar2=-1.0,
                                                       op0=OP.min, op1=OP.max), inc="dve")
                w1s_tok[jc] = qdve_tok[(jc, 0)]
                w1wr_tok[jc] = P.op("gpsimd", dma(w1q_scr[jc], W1S[par][:]),
                                    wait=[qdve_tok[(jc, 0)]], inc="gs", n=16)
                P.op("vector",
                     lambda e: e.tensor_scalar(out=QIA[:], in0=QFA[1][:],
                                               scalar1=S_COL[:, 1:2], scalar2=None,
                                               op0=OP.mult),
                     wait=[tq2, w2wr_tok.get(jc - 2)], inc="dve")
                qdve_tok[(jc, 1)] = P.op(
                    "vector",
                    lambda e, par=par: e.tensor_scalar(out=W2S[par][:], in0=QIA[:],
                                                       scalar1=1.0, scalar2=-1.0,
                                                       op0=OP.min, op1=OP.max), inc="dve")
                w2s_tok[jc] = qdve_tok[(jc, 1)]
                w2wr_tok[jc] = P.op("gpsimd", dma(w2q_scr[jc], W2S[par][:]),
                                    wait=[qdve_tok[(jc, 1)]], inc="gs", n=16)
                if 2 <= jc < 14:
                    emit_w3_step()
            else:
                wL = [ff1_tok.get(jc - 2)]
                if jc < 2:
                    wL.extend([prev_tile.get("ff143"), prev_tile.get("w1wr"),
                               prev_tile.get("w2wr")])
                P.op("sync", dma(W1S[par][:], w1q_scr[jc]),
                     wait=wL, inc=f"sL{par}", n=16)
                tl2 = P.op("sync", dma(W2S[par][:], w2q_scr[jc]), inc=f"sL{par}", n=16)
                w1s_tok[jc] = tl2
                w2s_tok[jc] = tl2
            for kc in range(KC):
                for h2 in range(2):
                    w1w = [w1s_tok[jc], xl2t_tok[kc],
                           sig_tok.get(jc - 2), hid_tok.get(jc - 2)]
                    if jc == 0 and kc == 0 and h2 == 0:
                        w1w.append(prev_tile.get("evac15"))
                    P.op("tensor",
                         lambda e, kc=kc, h2=h2, par=par: e.matmul(
                             X1PS[par][:, h2 * 512:(h2 + 1) * 512],
                             W1S[par][:, kc * 128:(kc + 1) * 128],
                             XLN2T[kc][:, h2 * 512:(h2 + 1) * 512],
                             start=(kc == 0), stop=(kc == KC - 1)),
                         wait=w1w, inc="pe")
                    ff1_tok[jc] = P.op(
                        "tensor",
                        lambda e, kc=kc, h2=h2, par=par: e.matmul(
                            X2PS[par][:, h2 * 512:(h2 + 1) * 512],
                            W2S[par][:, kc * 128:(kc + 1) * 128],
                            XLN2T[kc][:, h2 * 512:(h2 + 1) * 512],
                            start=(kc == 0), stop=(kc == KC - 1)),
                        wait=[w2s_tok[jc]], inc="pe")
            sig_tok[jc] = P.op("scalar",
                               lambda e, par=par: e.activation(out=SIG[par][:],
                                                               in_=X1PS[par][:],
                                                               func=AF.Sigmoid),
                               wait=[ff1_tok[jc]], inc="act")
            if jc > 0:
                emit_hid(jc - 1)
        emit_hid(JC - 1)
        if DBG and tile == 0:
            for jc in range(JC):
                P.op("gpsimd", dma(hid_dbg[jc], HID[jc][:]), wait=[hid_tok[jc]],
                     inc="gs", n=16)
        f1_ps.close()

        if tile == 0:
            w3pq["qfa0free"] = qdve_tok[(42, 0)]
        f2_ps = contextlib.ExitStack()
        OPS = [f2_ps.enter_context(nc.psum_tensor(f"OPS{tile}{i}", [128, 1024], F32))
               for i in range(4)]
        w3s_tok = {}
        omm_tok = {}
        xps_tok = {}
        tout_tok = {}

        def emit_out(fc, omm_tok=omm_tok, xps_tok=xps_tok, tout_tok=tout_tok,
                     OPS=OPS, tile=tile, tc0=tc0):
            par = fc % 2
            wout = [omm_tok[fc], xps_tok[fc], out_tok.get((tile, fc - 2))]
            if tile == 0 and fc < 2:
                wout.append(w3pq["mult"][10 + fc])
            tout_tok[fc] = P.op("vector",
                                lambda e, par=par, fc=fc: e.tensor_tensor(
                                    out=OUT_SB[par][:], in0=OPS[fc % 4][:],
                                    in1=QFA[1 + par][:, 0:1024], op=OP.add),
                                wait=wout, inc="dve")
            out_tok[(tile, fc)] = P.op("gpsimd",
                                       dma(y[fc * 128:(fc + 1) * 128, tc0:tc0 + 1024],
                                           OUT_SB[par][:]),
                                       wait=[tout_tok[fc]], inc="gs", n=16)

        for fc in range(16):
            par = fc % 2
            if tile == 0:
                w3s_tok[fc] = w3pq["asm"][fc]
            else:
                w3 = [omm_tok.get(fc - 2), w3pq["st"][fc]]
                if fc < 2:
                    w3.append(prev_tile.get("omm14" if fc == 0 else "omm15"))
                w3s_tok[fc] = P.op("sync", dma(W3S[par][:], w3q_scr[fc]),
                                   wait=w3, inc=f"s3{par}", n=16)
            wxps = [out_tok.get((tile, fc - 2))]
            if tile == 0 and fc < 2:
                wxps.append(qdve_tok.get((JC - 1, 1 - par)))
            xps_tok[fc] = P.op("sync",
                               dma(QFA[1 + par][:, 0:1024], xp_scr[fc][:, tc0:tc0 + 1024]),
                               wait=wxps, inc=f"sp{par}", n=16)
            for jc in range(JC):
                for h2 in range(2):
                    wo = [w3s_tok[fc], hid_tok[JC - 1],
                          out_tok.get((tile, fc - 4))]
                    if fc == 0 and jc == 0:
                        wo.append(sig_tok[JC - 1])
                    omm_tok[fc] = P.op(
                        "tensor",
                        lambda e, jc=jc, h2=h2, par=par, fc=fc: e.matmul(
                            OPS[fc % 4][:, h2 * 512:(h2 + 1) * 512],
                            W3S[par][:, jc * 128:(jc + 1) * 128],
                            HID[jc][:, h2 * 512:(h2 + 1) * 512],
                            start=(jc == 0), stop=(jc == JC - 1)),
                        wait=wo, inc="pe")
            if fc > 0:
                emit_out(fc - 1)
            if tile == 0 and fc < 14:
                w3pq["ommref"][fc] = omm_tok[fc]
                for _ in range(6):
                    emit_w3_step()
        emit_out(15)
        prev_tile = {"omm14": omm_tok.get(14), "omm15": omm_tok.get(15),
                     "w1wr": w1wr_tok.get(JC - 1), "w2wr": w2wr_tok.get(JC - 1),
                     "ff143": ff1_tok.get(JC - 1),
                     "evac15": tout_tok.get(15)}
        f2_ps.close()

    f_es.close()
    es.close()

    sem_es = contextlib.ExitStack()
    sems = {s: sem_es.enter_context(nc.semaphore(s)) for s in sorted(P.cnt)}
    with sem_es:
        with nc.Block() as block:
            def runner(name):
                def f(e):
                    for wl, fn, inc, n in P.ops[name]:
                        for (s, v) in wl:
                            e.wait_ge(sems[s], v)
                        if fn is not None:
                            ins = fn(e)
                            if inc is not None:
                                ins.then_inc(sems[inc], n)
                    if name in ("gpsimd", "sync"):
                        for s, v in P.cnt.items():
                            e.wait_ge(sems[s], v)
                return f
            block.sync(runner("sync"))
            block.gpsimd(runner("gpsimd"))
            block.tensor(runner("tensor"))
            block.scalar(runner("scalar"))
            block.vector(runner("vector"))
    return nc


def host_inputs(L0_attn_x, L0_attn_w_qkv, L0_attn_w_proj, L0_attn_b_proj,
                L0_ffn_w1, L0_ffn_w2, L0_ffn_w3):
    wq = np.asarray(L0_attn_w_qkv, np.float32)
    WqT = wq[0:2048].T
    K = wq[2048:4096].T
    V = wq[4096:6144].T
    wq_blk = np.ascontiguousarray(
        WqT.reshape(16, 128, 16, 128).transpose(2, 1, 0, 3)).reshape(16, 128, 2048)
    slabs = []
    for gg in range(8):
        g0, g1 = 2 * gg, 2 * gg + 1
        slab = np.concatenate([K[:, g0 * 128:(g0 + 1) * 128], K[:, g1 * 128:(g1 + 1) * 128],
                               V[:, g0 * 128:(g0 + 1) * 128], V[:, g1 * 128:(g1 + 1) * 128]],
                              axis=1)
        slabs.append(np.ascontiguousarray(slab).reshape(16, 128, 512))
    wkv_blk = np.ascontiguousarray(np.stack(slabs))
    WpT = np.asarray(L0_attn_w_proj, np.float32).T
    wpj_blk = np.ascontiguousarray(
        WpT.reshape(16, 128, 16, 128).transpose(2, 1, 0, 3)).reshape(16, 128, 2048)
    W1T = np.asarray(L0_ffn_w1, np.float32).T
    w1_blk = np.ascontiguousarray(
        W1T.reshape(16, 128, 44, 128).transpose(2, 1, 0, 3)).reshape(44, 128, 2048)
    W2T = np.asarray(L0_ffn_w2, np.float32).T
    w2_blk = np.ascontiguousarray(
        W2T.reshape(16, 128, 44, 128).transpose(2, 1, 0, 3)).reshape(44, 128, 2048)
    W3T = np.asarray(L0_ffn_w3, np.float32).T
    w3_blk = np.ascontiguousarray(
        W3T.reshape(44, 128, 16, 128).transpose(2, 1, 0, 3)).reshape(16, 128, 5632)
    bcol = np.ascontiguousarray(np.asarray(L0_attn_b_proj, np.float32).reshape(16, 128).T)
    ones_stat = np.full((128, 128), 1.0 / 2048.0, np.float32)
    ones_col = np.ones((128, 1), np.float32)
    onesrow = np.ones((1, 128), np.float32)
    mask = np.zeros((128, 128), np.float32)
    for h in range(4):
        mask[h * 32:(h + 1) * 32, h * 32:(h + 1) * 32] = 1.0
    epss = np.zeros((128, 2), np.float32)
    epss[:, 0] = 1e-5
    epss[:, 1] = 1e-15
    shared = dict(wq_blk=wq_blk, wkv_blk=wkv_blk, wpj_blk=wpj_blk, w1_blk=w1_blk,
                  w2_blk=w2_blk, w3_blk=w3_blk, bcol=bcol, ones_stat=ones_stat,
                  ones_col=ones_col, onesrow=onesrow, mask_bd=mask, epss=epss)
    x = np.asarray(L0_attn_x, np.float32)
    in_maps = []
    for b in range(8):
        m = dict(shared)
        m["xfm"] = np.ascontiguousarray(x[b].T)
        in_maps.append(m)
    return in_maps


def kernel(L0_attn_x, L0_attn_w_qkv, L0_attn_w_proj, L0_attn_b_proj,
           L0_ffn_w1, L0_ffn_w2, L0_ffn_w3):
    in_maps = host_inputs(L0_attn_x, L0_attn_w_qkv, L0_attn_w_proj, L0_attn_b_proj,
                          L0_ffn_w1, L0_ffn_w2, L0_ffn_w3)
    nc = build()
    res = run_bass_kernel_spmd(nc, in_maps, core_ids=list(range(8)))
    out = np.empty((8, T, D), np.float32)
    for b in range(8):
        out[b] = np.asarray(res.results[b]["y"]).T
    return out


# version 21
# speedup vs baseline: 1.0602x; 1.0100x over previous
import contextlib
import numpy as np
import concourse.bass as bass
import concourse.mybir as mybir
from concourse.bass_utils import run_bass_kernel_spmd
from concourse.alu_op_type import AluOpType as OP

F32 = mybir.dt.float32
BF16 = mybir.dt.bfloat16
I32 = mybir.dt.int32
AF = mybir.ActivationFunctionType
AX = mybir.AxisListType.X

ENGINES = ("sync", "gpsimd", "tensor", "scalar", "vector")

D = 2048
T = 2048
KC = 16
H = 5632
JC = 44
NW = float(H * D)
DBG = False


class Prog:
    def __init__(self):
        self.ops = {e: [] for e in ENGINES}
        self.cnt = {}
        self.lastw = {e: {} for e in ENGINES}

    def op(self, eng, fn, wait=(), inc=None, n=1):
        wl = []
        for t in wait:
            if t is None:
                continue
            s, v = t
            if self.lastw[eng].get(s, 0) < v:
                wl.append((s, v))
                self.lastw[eng][s] = v
        tok = None
        if inc is not None:
            self.cnt[inc] = self.cnt.get(inc, 0) + n
            tok = (inc, self.cnt[inc])
        self.ops[eng].append((wl, fn, inc, n))
        return tok

    def barrier(self):
        toks = list(self.cnt.items())
        for e in ENGINES:
            wl = []
            for s, v in toks:
                if self.lastw[e].get(s, 0) < v:
                    wl.append((s, v))
                    self.lastw[e][s] = v
            if wl:
                self.ops[e].append((wl, None, None, 0))


def dma(dst, src):
    return lambda e: e.dma_start(out=dst, in_=src)


def build():
    nc = bass.Bass("TRN2", target_bir_lowering=False)
    P = Prog()

    xfm = nc.dram_tensor("xfm", [D, T], F32, kind="ExternalInput").ap()
    wq_blk = nc.dram_tensor("wq_blk", [16, 128, 2048], F32, kind="ExternalInput").ap()
    wkv_blk = nc.dram_tensor("wkv_blk", [8, 16, 128, 512], F32, kind="ExternalInput").ap()
    wpj_blk = nc.dram_tensor("wpj_blk", [16, 128, 2048], F32, kind="ExternalInput").ap()
    w1_blk = nc.dram_tensor("w1_blk", [44, 128, 2048], F32, kind="ExternalInput").ap()
    w2_blk = nc.dram_tensor("w2_blk", [44, 128, 2048], F32, kind="ExternalInput").ap()
    w3_blk = nc.dram_tensor("w3_blk", [16, 128, 5632], F32, kind="ExternalInput").ap()
    bcol_in = nc.dram_tensor("bcol", [128, 16], F32, kind="ExternalInput").ap()
    ones_stat_in = nc.dram_tensor("ones_stat", [128, 128], F32, kind="ExternalInput").ap()
    ones_col_in = nc.dram_tensor("ones_col", [128, 1], F32, kind="ExternalInput").ap()
    onesrow_in = nc.dram_tensor("onesrow", [1, 128], F32, kind="ExternalInput").ap()
    mask_in = nc.dram_tensor("mask_bd", [128, 128], F32, kind="ExternalInput").ap()
    epss_in = nc.dram_tensor("epss", [128, 2], F32, kind="ExternalInput").ap()

    y = nc.dram_tensor("y", [D, T], F32, kind="ExternalOutput").ap()
    sk = dict(kind="ExternalOutput") if DBG else {}
    xp_scr = nc.dram_tensor("xp_scr", [16, 128, T], F32, **sk).ap()
    xl2_scr = nc.dram_tensor("xl2_scr", [16, 128, T], BF16, **sk).ap()
    w1q_scr = nc.dram_tensor("w1q_scr", [44, 128, 2048], BF16, **sk).ap()
    w2q_scr = nc.dram_tensor("w2q_scr", [44, 128, 2048], BF16, **sk).ap()
    w3q_scr = nc.dram_tensor("w3q_scr", [16, 128, 5632], BF16, **sk).ap()
    if DBG:
        xln_dbg = nc.dram_tensor("xln_dbg", [16, 128, T], BF16, kind="ExternalOutput").ap()
        vk_dbg = nc.dram_tensor("vk_dbg", [16, 128, 128], BF16, kind="ExternalOutput").ap()
        ks_dbg = nc.dram_tensor("ks_dbg", [16, 128, 128], BF16, kind="ExternalOutput").ap()
        attn_dbg = nc.dram_tensor("attn_dbg", [2, 16, 128, 1024], BF16,
                                  kind="ExternalOutput").ap()
        scl_dbg = nc.dram_tensor("scl_dbg", [128, 4], F32, kind="ExternalOutput").ap()
        hid_dbg = nc.dram_tensor("hid_dbg", [44, 128, 1024], BF16, kind="ExternalOutput").ap()

    es = contextlib.ExitStack()
    sb = lambda name, shape, dt: es.enter_context(nc.sbuf_tensor(name, shape, dt))

    MASK = sb("MASK", [128, 128], F32)
    ONES_ST = sb("ONES_ST", [128, 128], BF16)
    ONES_CB = sb("ONES_CB", [128, 1], BF16)
    ONES_CF = sb("ONES_CF", [128, 1], F32)
    ONESROW = sb("ONESROW", [1, 128], F32)
    GACC = sb("GACC", [128, 272], F32)
    SCLA = sb("SCLA", [1, 4], F32)
    SCLB = sb("SCLB", [1, 4], F32)
    S_COL = sb("S_COL", [128, 4], F32)
    EPS_SB = sb("EPS_SB", [128, 2], F32)

    ab_es = contextlib.ExitStack()
    sba = lambda name, shape, dt: ab_es.enter_context(nc.sbuf_tensor(name, shape, dt))
    XLN = [sba(f"XLN{i}", [128, T], BF16) for i in range(KC)]
    VKBLK = [sba(f"VKB{i}", [128, 128], BF16) for i in range(16)]
    KSBLK = [sba(f"KSB{i}", [128, 128], BF16) for i in range(16)]
    B_COL = sba("B_COL", [128, 16], F32)
    MEAN_SB = sba("MEAN_SB", [128, 1024], F32)
    M2_SB = sba("M2_SB", [128, 1024], F32)
    VAR_SB = sba("VAR_SB", [128, 1024], F32)
    STD_SB = sba("STD_SB", [128, 1024], F32)
    RSTD_SB = sba("RSTD_SB", [128, 1024], F32)
    TMPF = sba("TMPF", [128, 1024], F32)
    GSUM = sba("GSUM", [128, 4], F32)

    P.op("sync", dma(MASK[:], mask_in), inc="sc", n=16)
    P.op("sync", dma(B_COL[:], bcol_in), inc="sc", n=16)
    P.op("sync", dma(ONESROW[:], onesrow_in), inc="sc", n=16)
    P.op("sync", dma(ONES_CF[:], ones_col_in), inc="sc", n=16)
    t_cs = P.op("sync", dma(EPS_SB[:], epss_in), inc="sc", n=16)
    P.op("gpsimd", dma(ONES_ST[:], ones_stat_in), inc="gc", n=16)
    t_cg = P.op("gpsimd", dma(ONES_CB[:], ones_col_in), inc="gc", n=16)

    a_es = contextlib.ExitStack()
    sbA = lambda name, shape, dt: a_es.enter_context(nc.sbuf_tensor(name, shape, dt))
    XTS = [sbA(f"XTS{i}", [128, 512], F32) for i in range(KC)]
    XB = [sbA(f"XB{i}", [128, 512], BF16) for i in range(2)]
    XSQ = [sbA(f"XSQ{i}", [128, 512], BF16) for i in range(2)]
    WKV = [sbA(f"WKV{i}", [128, 16 * 512], BF16) for i in range(2)]
    WKVF = [sbA(f"WKVF{i}", [128, 2048], F32) for i in range(2)]
    GBUF = [sbA(f"GBUF{i}", [128, 1024], F32) for i in range(4)]
    KT = [sbA(f"KT{i}", [128, 256], BF16) for i in range(3)]
    VT = [sbA(f"VT{i}", [128, 256], BF16) for i in range(3)]
    KS_SB = sbA("KS_SB", [128, 16], F32)

    gsteps = []
    for jc in range(44):
        for c0 in (0, 1024):
            gsteps.append((w1_blk[jc][:, c0:c0 + 1024], 1024))
    for jc in range(44):
        for c0 in (0, 1024):
            gsteps.append((w2_blk[jc][:, c0:c0 + 1024], 1024))
    for fc in range(16):
        for c0 in range(0, 5632, 1024):
            w = min(1024, 5632 - c0)
            gsteps.append((w3_blk[fc][:, c0:c0 + w], w))
    n_gsteps = len(gsteps)
    g_state = {"i": 0, "acttok": {}}

    def emit_gamma_step():
        i = g_state["i"]
        if i >= n_gsteps:
            return
        g_state["i"] = i + 1
        src, w = gsteps[i]
        t_ld = P.op("sync", dma(GBUF[i % 4][:, :w], src),
                    wait=[g_state["acttok"].get(i - 4)], inc=f"sg{i % 4}", n=16)

        def f(e, i=i, w=w):
            return e.tensor_reduce(out=GACC[:, i:i + 1], in_=GBUF[i % 4][:, :w],
                                   axis=AX, op=OP.add, apply_absolute_value=True)
        g_state["acttok"][i] = P.op("vector", f, wait=[t_ld], inc="dve")

    kvmm_tok = {}
    wkv_tok = {}
    wkvf_cast = {}
    wkv_ld = {}

    def emit_wkv_load(gg, gi):
        i = gg * 4 + gi
        k0 = 4 * gi
        srcap = wkv_blk[gg, k0:k0 + 4].rearrange("k p f -> p k f")
        stg = WKVF[i % 2][:].rearrange("p (k f) -> p k f", k=4)
        wkv_ld[i] = P.op("gpsimd", dma(stg, srcap), wait=[wkvf_cast.get(i - 2)],
                         inc=f"skv{i % 2}", n=16)

    def emit_wkv_cast(gg, gi):
        i = gg * 4 + gi
        k0 = 4 * gi
        par = gg % 2
        wkvf_cast[i] = P.op("scalar",
                            lambda e, par=par, k0=k0, i=i: e.activation(
                                out=WKV[par][:, k0 * 512:(k0 + 4) * 512],
                                in_=WKVF[i % 2][:], func=AF.Copy),
                            wait=[wkv_ld[i], kvmm_tok.get((gg - 2, 15))], inc="act")
        if gi == 3:
            wkv_tok[gg] = wkvf_cast[i]

    emit_wkv_load(0, 0)
    emit_wkv_load(0, 1)

    a1_ps = contextlib.ExitStack()
    MEAN_PS = a1_ps.enter_context(nc.psum_tensor("MEAN_PS", [128, 512], F32))
    MSQ_PS = a1_ps.enter_context(nc.psum_tensor("MSQ_PS", [128, 512], F32))

    apply_tok = {}
    xln_tok = {}
    mm_stat_tok = {}
    psum_free_tok = {}
    for ts in range(4):
        c0 = ts * 512
        txs = {}
        for kc in range(KC):
            txs[kc] = P.op("sync", dma(XTS[kc][:], xfm[kc * 128:(kc + 1) * 128, c0:c0 + 512]),
                           wait=[apply_tok.get((ts - 1, kc))], inc=f"sx{kc}", n=16)
        cast_tok = {}
        for kc in range(KC):
            cast_tok[(kc, 0)] = P.op(
                "scalar",
                lambda e, kc=kc: e.activation(out=XB[kc % 2][:], in_=XTS[kc][:], func=AF.Copy),
                wait=[txs[kc], mm_stat_tok.get((ts, kc - 2, 0))], inc="act")
            cast_tok[(kc, 1)] = P.op(
                "scalar",
                lambda e, kc=kc: e.activation(out=XSQ[kc % 2][:], in_=XTS[kc][:],
                                              func=AF.Square),
                wait=[mm_stat_tok.get((ts, kc - 2, 1))], inc="act")
            war_ps = psum_free_tok.get(ts - 1) if kc == 0 else None
            mm_stat_tok[(ts, kc, 0)] = P.op(
                "tensor",
                lambda e, kc=kc: e.matmul(MEAN_PS[:], ONES_ST[:], XB[kc % 2][:],
                                          start=(kc == 0), stop=(kc == KC - 1)),
                wait=[cast_tok[(kc, 0)], t_cg, war_ps], inc="pe")
            mm_stat_tok[(ts, kc, 1)] = P.op(
                "tensor",
                lambda e, kc=kc: e.matmul(MSQ_PS[:], ONES_ST[:], XSQ[kc % 2][:],
                                          start=(kc == 0), stop=(kc == KC - 1)),
                wait=[cast_tok[(kc, 1)]], inc="pe")
        emit_wkv_cast(0, ts)
        if ts < 2:
            emit_wkv_load(0, ts + 2)
        else:
            emit_wkv_load(1, ts - 2)
        t1 = P.op("vector", lambda e: e.tensor_copy(out=MEAN_SB[:, :512], in_=MEAN_PS[:]),
                  wait=[mm_stat_tok[(ts, KC - 1, 0)]], inc="dve")
        t2 = P.op("scalar", lambda e: e.activation(out=M2_SB[:, :512], in_=MEAN_SB[:, :512],
                                                   func=AF.Square), wait=[t1], inc="act")
        t3 = P.op("vector", lambda e: e.tensor_tensor(out=VAR_SB[:, :512], in0=MSQ_PS[:],
                                                      in1=M2_SB[:, :512], op=OP.subtract),
                  wait=[mm_stat_tok[(ts, KC - 1, 1)], t2], inc="dve")
        psum_free_tok[ts] = t3
        t4 = P.op("scalar", lambda e: e.activation(out=STD_SB[:, :512], in_=VAR_SB[:, :512],
                                                   func=AF.Ln, bias=EPS_SB[:, 0:1]),
                  wait=[t3, t_cs], inc="act")
        t5 = P.op("scalar", lambda e: e.activation(out=RSTD_SB[:, :512], in_=STD_SB[:, :512],
                                                   func=AF.Exp, scale=-0.5),
                  wait=[t4], inc="act")
        for kc in range(KC):
            P.op("vector",
                 lambda e, kc=kc: e.tensor_tensor(out=TMPF[:, :512], in0=XTS[kc][:],
                                                  in1=MEAN_SB[:, :512], op=OP.subtract),
                 wait=[t5], inc="dve")
            tok = P.op("vector",
                       lambda e, kc=kc, c0=c0: e.tensor_tensor(
                           out=XLN[kc][:, c0:c0 + 512], in0=TMPF[:, :512],
                           in1=RSTD_SB[:, :512], op=OP.mult), inc="dve")
            apply_tok[(ts, kc)] = tok
            xln_tok[(ts, kc)] = tok
        for _ in range(4):
            emit_gamma_step()

    a1_ps.close()

    a2_ps = contextlib.ExitStack()
    KVPS = [a2_ps.enter_context(nc.psum_tensor(f"KVPS{i}", [128, 512], F32)) for i in range(3)]
    VKPS = a2_ps.enter_context(nc.psum_tensor("VKPS", [128, 2048], F32))
    KSPS = a2_ps.enter_context(nc.psum_tensor("KSPS", [128, 512], F32))

    def prev_r_use(gg, tt):
        if tt >= 3:
            return (gg, tt - 3)
        if gg == 0:
            return None
        return (gg - 1, 15 if tt == 0 else tt + 12)

    evac_tok = {}
    eins_tok = {}
    pend_eins = None

    def emit_eins(gg, tt, r):
        for g01 in range(2):
            g = 2 * gg + g01
            P.op("tensor",
                 lambda e, r=r, g01=g01, g=g, tt=tt: e.matmul(
                     VKPS[:, g * 128:(g + 1) * 128], KT[r][:, g01 * 128:(g01 + 1) * 128],
                     VT[r][:, g01 * 128:(g01 + 1) * 128],
                     start=(tt == 0 and g % 4 == 0), stop=(tt == 15)),
                 wait=[evac_tok[(gg, tt)], t_cg], inc="pe")
            eins_tok[(gg, tt)] = P.op(
                "tensor",
                lambda e, r=r, g01=g01, g=g, tt=tt: e.matmul(
                    KSPS[:, g:g + 1], KT[r][:, g01 * 128:(g01 + 1) * 128], ONES_CB[:],
                    start=(tt == 0 and g == 0), stop=(tt == 15)),
                inc="pe")

    for gg in range(8):
        par = gg % 2
        for tt in range(16):
            r = tt % 3
            if gg < 7:
                if tt == 0:
                    emit_wkv_load(gg + 1, 0)
                elif tt == 1:
                    emit_wkv_load(gg + 1, 1)
                elif tt == 4:
                    emit_wkv_cast(gg + 1, 0)
                elif tt == 5:
                    emit_wkv_load(gg + 1, 2)
                elif tt == 7:
                    emit_wkv_cast(gg + 1, 1)
                elif tt == 8:
                    emit_wkv_load(gg + 1, 3)
                elif tt == 10:
                    emit_wkv_cast(gg + 1, 2)
                elif tt == 13:
                    emit_wkv_cast(gg + 1, 3)
            for kc in range(KC):
                w = [xln_tok[(tt // 4, kc)], wkv_tok[gg]]
                if kc == 0:
                    w.append(evac_tok.get(prev_r_use(gg, tt)))
                    if gg == 0 and tt < 3:
                        w.append(psum_free_tok[3])
                kvmm_tok[(gg, tt)] = P.op(
                    "tensor",
                    lambda e, kc=kc, r=r, tt=tt, par=par: e.matmul(
                        KVPS[r][:], XLN[kc][:, tt * 128:(tt + 1) * 128],
                        WKV[par][:, kc * 512:(kc + 1) * 512],
                        start=(kc == 0), stop=(kc == KC - 1)),
                    wait=w, inc="pe")
            if pend_eins is not None:
                emit_eins(*pend_eins)
            wE = [kvmm_tok[(gg, tt)], eins_tok.get(prev_r_use(gg, tt))]
            P.op("scalar", lambda e, r=r: e.activation(out=KT[r][:], in_=KVPS[r][:, 0:256],
                                                       func=AF.Relu), wait=wE, inc="act")
            evac_tok[(gg, tt)] = P.op(
                "scalar", lambda e, r=r: e.activation(out=VT[r][:], in_=KVPS[r][:, 256:512],
                                                      func=AF.Copy), inc="act")
            pend_eins = (gg, tt, r)
            emit_gamma_step()
            if tt % 2 == 0:
                emit_gamma_step()
    emit_eins(*pend_eins)

    vk_done = eins_tok[(7, 15)]

    t_ks = P.op("vector", lambda e: e.tensor_copy(out=KS_SB[:], in_=KSPS[:, 0:16]),
                wait=[vk_done, t_cs], inc="dve")
    vkblk_tok = {}
    for g in range(16):
        P.op("vector",
             lambda e, g=g: e.tensor_tensor(out=VKBLK[g][:],
                                            in0=VKPS[:, g * 128:(g + 1) * 128],
                                            in1=MASK[:], op=OP.mult), inc="dve")
        vkblk_tok[g] = P.op("vector",
                            lambda e, g=g: e.tensor_scalar(out=KSBLK[g][:], in0=MASK[:],
                                                           scalar1=KS_SB[:, g:g + 1],
                                                           scalar2=None, op0=OP.mult),
                            inc="dve")
    if DBG:
        for g in range(16):
            P.op("gpsimd", dma(vk_dbg[g], VKBLK[g][:]), wait=[vkblk_tok[g]], inc="gs", n=16)
            P.op("gpsimd", dma(ks_dbg[g], KSBLK[g][:]), inc="gs", n=16)
        for kc in range(KC):
            P.op("gpsimd", dma(xln_dbg[kc], XLN[kc][:]), inc="gs", n=16)
    a2_ps.close()

    P.barrier()
    a_es.close()

    b_es = contextlib.ExitStack()
    sbB = lambda name, shape, dt: b_es.enter_context(nc.sbuf_tensor(name, shape, dt))
    ATTN = [sbB(f"ATTN{i}", [128, 1024], BF16) for i in range(16)]
    WQS = [sbB(f"WQS{i}", [128, 2048], BF16) for i in range(2)]
    WPS = [sbB(f"WPS{i}", [128, 2048], BF16) for i in range(2)]
    QSB = [sbB(f"QSB{i}", [128, 1024], BF16) for i in range(2)]
    DEN_SB = sbB("DEN_SB", [128, 1024], F32)
    RCP_SB = sbB("RCP_SB", [128, 1024], F32)
    XRES = [sbB(f"XRES{i}", [128, 1024], F32) for i in range(2)]
    XP_SB = [sbB(f"XP{i}", [128, 1024], F32) for i in range(2)]
    XB2 = [sbB(f"XB2_{i}", [128, 1024], BF16) for i in range(2)]
    XSQ2 = [sbB(f"XSQ2_{i}", [128, 1024], BF16) for i in range(2)]
    XL2 = [sbB(f"XL2_{i}", [128, 1024], BF16) for i in range(2)]
    WSTG = [sbB(f"WSTG{i}", [128, 2048], F32) for i in range(2)]

    xp_store_tok = {}
    gb_lastload = {}

    def emit_gamma_step_b(extra=(), pair=None):
        i = g_state["i"]
        if i >= n_gsteps:
            return
        g_state["i"] = i + 1
        src, w = gsteps[i]
        p2 = i % len(pair)
        stg = pair[p2]
        lw = [g_state["acttok"].get(i - len(pair))] + list(extra)
        t_ld = P.op("gpsimd", dma(stg[:, 0:w], src), wait=lw, inc=f"gb{p2}", n=16)
        gb_lastload[p2] = t_ld

        def f(e, i=i, w=w, stg=stg):
            return e.tensor_reduce(out=GACC[:, i:i + 1], in_=stg[:, 0:w],
                                   axis=AX, op=OP.add, apply_absolute_value=True)
        g_state["acttok"][i] = P.op("vector", f, wait=[t_ld], inc="dve")

    def emit_apply_chunk(actx):
        if actx is None:
            return
        kc = actx["kc"]
        if kc >= KC:
            return
        actx["kc"] = kc + 1
        th_, h0_, nL = actx["th"], actx["h0"], actx["nL"]
        s = kc % nL
        par = kc % 2
        P.op("vector",
             lambda e, actx=actx, s=s: e.tensor_tensor(out=TMPF[:], in0=actx["rslot"][s],
                                                       in1=MEAN_SB[:], op=OP.subtract),
             wait=[actx["xpb"][kc], actx["t5"], actx["xl2st"].get(kc - 2)], inc="dve")
        actx["xl2"][kc] = P.op("vector",
                               lambda e, par=par: e.tensor_tensor(out=XL2[par][:],
                                                                  in0=TMPF[:],
                                                                  in1=RSTD_SB[:],
                                                                  op=OP.mult),
                               inc="dve")
        if kc + nL < KC:
            s2 = (kc + nL) % nL
            t_l = P.op("gpsimd",
                       dma(actx["rslot"][s2], xp_scr[kc + nL][:, h0_:h0_ + 1024]),
                       wait=[xp_store_tok[(th_, 15)], actx["xl2"][kc]],
                       inc=f"bx{th_}{s2}", n=16)
            actx["xpb"][kc + nL] = t_l
            if s2 == 4:
                actx["var_last_ld"] = t_l
        actx["xl2st"][kc] = P.op("gpsimd",
                                 dma(xl2_scr[kc][:, h0_:h0_ + 1024], XL2[par][:]),
                                 wait=[actx["xl2"][kc]], inc=f"gx{par}", n=16)

    prev_th = {}
    prev_actx = None
    for th in range(2):
        h0 = th * 1024
        pa = prev_actx
        gb_pair = (TMPF, MEAN_SB, M2_SB, STD_SB)
        gb_cnt = [0]
        bq_ps = contextlib.ExitStack()
        QPS = [bq_ps.enter_context(nc.psum_tensor(f"QPS{th}{i}", [128, 1024], F32))
               for i in range(2)]
        NUMPS = bq_ps.enter_context(nc.psum_tensor(f"NUMPS{th}", [128, 1024], F32))
        DENPS = bq_ps.enter_context(nc.psum_tensor(f"DENPS{th}", [128, 1024], F32))
        wqs_tok = {}
        qmm_tok = {}
        relu_tok = {}
        nd_tok = {}
        den_tok = {}
        rcp_tok = {}
        attn_tok = {}

        def emit_numden(qc, attn_tok=attn_tok, relu_tok=relu_tok, nd_tok=nd_tok,
                        den_tok=den_tok, rcp_tok=rcp_tok, NUMPS=NUMPS, DENPS=DENPS):
            par = qc % 2
            for h2 in range(2):
                wn = [relu_tok[qc], attn_tok.get(qc - 1), vkblk_tok[15]]
                if qc == 0:
                    wn.append(prev_th.get("stats_free"))
                P.op("tensor",
                     lambda e, qc=qc, h2=h2, par=par: e.matmul(
                         NUMPS[:, h2 * 512:(h2 + 1) * 512], VKBLK[qc][:],
                         QSB[par][:, h2 * 512:(h2 + 1) * 512], start=True, stop=True),
                     wait=wn, inc="pe")
                nd_tok[qc] = P.op(
                    "tensor",
                    lambda e, qc=qc, h2=h2, par=par: e.matmul(
                        DENPS[:, h2 * 512:(h2 + 1) * 512], KSBLK[qc][:],
                        QSB[par][:, h2 * 512:(h2 + 1) * 512], start=True, stop=True),
                    wait=[den_tok.get(qc - 1)], inc="pe")
            den_tok[qc] = P.op("scalar",
                               lambda e: e.activation(out=DEN_SB[:], in_=DENPS[:],
                                                      func=AF.Ln, bias=EPS_SB[:, 1:2]),
                               wait=[nd_tok[qc], rcp_tok.get(qc - 1), t_cs], inc="act")
            rcp_tok[qc] = P.op("scalar",
                               lambda e: e.activation(out=RCP_SB[:], in_=DEN_SB[:],
                                                      func=AF.Exp, scale=-1.0),
                               wait=[attn_tok.get(qc - 1)], inc="act")
            attn_tok[qc] = P.op("vector",
                                lambda e, qc=qc: e.tensor_tensor(out=ATTN[qc][:], in0=NUMPS[:],
                                                                 in1=RCP_SB[:], op=OP.mult),
                                wait=[rcp_tok[qc]], inc="dve")

        def emit_wq_load(qc, wqs_tok=wqs_tok, qmm_tok=qmm_tok, wstg_prev=None,
                         twq_pre=None):
            par = qc % 2
            if twq_pre is not None:
                twq = twq_pre
            else:
                twq = P.op("sync", dma(WSTG[par][:], wq_blk[qc]),
                           wait=[wqs_tok.get(qc - 2), wstg_prev], inc=f"sw{par}", n=16)
            wqs_tok[qc] = P.op("scalar",
                               lambda e, par=par: e.activation(out=WQS[par][:],
                                                               in_=WSTG[par][:],
                                                               func=AF.Copy),
                               wait=[twq, qmm_tok.get(qc - 2),
                                     prev_th.get("qps_free")], inc="act")

        emit_wq_load(0, wstg_prev=prev_th.get("wstg0"),
                     twq_pre=prev_th.get("pre_wq0"))
        emit_wq_load(1, wstg_prev=prev_th.get("wstg1"),
                     twq_pre=prev_th.get("pre_wq1"))
        for qc in range(16):
            par = qc % 2
            for kc in range(KC):
                for h2 in range(2):
                    wq = [wqs_tok[qc], relu_tok.get(qc - 2)]
                    if qc == 0 and kc == 0:
                        wq.append(prev_th.get("pps_free"))
                        wq.append(prev_th.get("stats_free"))
                    qmm_tok[qc] = P.op(
                        "tensor",
                        lambda e, kc=kc, h2=h2, par=par, h0=h0: e.matmul(
                            QPS[par][:, h2 * 512:(h2 + 1) * 512],
                            WQS[par][:, kc * 128:(kc + 1) * 128],
                            XLN[kc][:, h0 + h2 * 512:h0 + (h2 + 1) * 512],
                            start=(kc == 0), stop=(kc == KC - 1)),
                        wait=wq, inc="pe")
            if qc + 1 < 16:
                emit_wq_load(qc + 1)
            if qc > 0:
                emit_numden(qc - 1)
            emit_apply_chunk(pa)
            emit_apply_chunk(pa)
            if th == 0:
                for _g in range(4):
                    emit_gamma_step_b((), gb_pair)
            relu_tok[qc] = P.op("scalar",
                                lambda e, par=par: e.activation(out=QSB[par][:],
                                                                in_=QPS[par][:], func=AF.Relu),
                                wait=[qmm_tok[qc], nd_tok.get(qc - 2)], inc="act")
        emit_numden(15)
        if DBG:
            for qc in range(16):
                P.op("gpsimd", dma(attn_dbg[th, qc], ATTN[qc][:]), wait=[attn_tok[qc]],
                     inc="gs", n=16)
        bq_ps.close()

        bp_ps = contextlib.ExitStack()
        PPS = [bp_ps.enter_context(nc.psum_tensor(f"PPS{th}{i}", [128, 1024], F32))
               for i in range(2)]
        MEAN2_PS = bp_ps.enter_context(nc.psum_tensor(f"MEAN2_{th}", [128, 1024], F32))
        MSQ2_PS = bp_ps.enter_context(nc.psum_tensor(f"MSQ2_{th}", [128, 1024], F32))
        wps_tok = {}
        pmm_tok = {}
        xres_tok = {}
        xp_tok = {}
        cast2_tok = {}
        smm_tok = {}

        def emit_stats(oc, cast2_tok=cast2_tok, smm_tok=smm_tok,
                       MEAN2_PS=MEAN2_PS, MSQ2_PS=MSQ2_PS,
                       attn_tok=attn_tok, den_tok=den_tok, rcp_tok=rcp_tok):
            par = oc % 2
            for h2 in range(2):
                ws = [cast2_tok[(oc, 0)]]
                if oc == 0:
                    ws.extend([attn_tok[15], den_tok[15], rcp_tok[15]])
                smm_tok[(oc, 0)] = P.op(
                    "tensor",
                    lambda e, h2=h2, oc=oc, par=par: e.matmul(
                        MEAN2_PS[:, h2 * 512:(h2 + 1) * 512], ONES_ST[:],
                        XB2[par][:, h2 * 512:(h2 + 1) * 512],
                        start=(oc == 0), stop=(oc == 15)),
                    wait=ws, inc="pe")
                smm_tok[(oc, 1)] = P.op(
                    "tensor",
                    lambda e, h2=h2, oc=oc, par=par: e.matmul(
                        MSQ2_PS[:, h2 * 512:(h2 + 1) * 512], ONES_ST[:],
                        XSQ2[par][:, h2 * 512:(h2 + 1) * 512],
                        start=(oc == 0), stop=(oc == 15)),
                    wait=[cast2_tok[(oc, 1)]], inc="pe")

        def emit_wp_load(oc, wps_tok=wps_tok, pmm_tok=pmm_tok, wqs_tok=wqs_tok):
            par = oc % 2
            twp = P.op("sync", dma(WSTG[par][:], wpj_blk[oc]),
                       wait=[wps_tok.get(oc - 2), wqs_tok[14 + par]], inc=f"sw{par}", n=16)
            wps_tok[oc] = P.op("scalar",
                               lambda e, par=par: e.activation(out=WPS[par][:],
                                                               in_=WSTG[par][:],
                                                               func=AF.Copy),
                               wait=[twp, pmm_tok.get(oc - 2)], inc="act")

        emit_wp_load(0)
        emit_wp_load(1)
        for oc in range(16):
            par = oc % 2
            xres_tok[oc] = P.op("sync",
                                dma(XRES[par][:], xfm[oc * 128:(oc + 1) * 128, h0:h0 + 1024]),
                                wait=[xp_tok.get(oc - 2),
                                      pa["xl2"].get(12 + oc)
                                      if (pa is not None and oc < 2) else None],
                                inc=f"sxr{par}", n=16)
            for ac in range(16):
                for h2 in range(2):
                    wp = [wps_tok[oc], attn_tok[15], xp_tok.get(oc - 2)]
                    if oc == 0 and ac == 0:
                        wp.extend([relu_tok[15], den_tok[15]])
                    pmm_tok[oc] = P.op(
                        "tensor",
                        lambda e, ac=ac, h2=h2, par=par: e.matmul(
                            PPS[par][:, h2 * 512:(h2 + 1) * 512],
                            WPS[par][:, ac * 128:(ac + 1) * 128],
                            ATTN[ac][:, h2 * 512:(h2 + 1) * 512],
                            start=(ac == 0), stop=(ac == 15)),
                        wait=wp, inc="pe")
            if oc + 1 < 16:
                emit_wp_load(oc + 1)
            if oc > 0:
                emit_stats(oc - 1)
            xp_tok[oc] = P.op("vector",
                              lambda e, oc=oc, par=par: e.scalar_tensor_tensor(
                                  out=XP_SB[par][:], in0=PPS[par][:],
                                  scalar=B_COL[:, oc:oc + 1], in1=XRES[par][:],
                                  op0=OP.add, op1=OP.add),
                              wait=[pmm_tok[oc], xres_tok[oc], t_cs,
                                    xp_store_tok.get((th, oc - 2)),
                                    cast2_tok.get((oc - 2, 1))], inc="dve")
            xp_store_tok[(th, oc)] = P.op("gpsimd",
                                          dma(xp_scr[oc][:, h0:h0 + 1024], XP_SB[par][:]),
                                          wait=[xp_tok[oc]], inc="gs", n=16)
            cast2_tok[(oc, 0)] = P.op("scalar",
                                      lambda e, par=par: e.activation(
                                          out=XB2[par][:], in_=XP_SB[par][:], func=AF.Copy),
                                      wait=[xp_tok[oc], smm_tok.get((oc - 2, 0))], inc="act")
            cast2_tok[(oc, 1)] = P.op("scalar",
                                      lambda e, par=par: e.activation(
                                          out=XSQ2[par][:], in_=XP_SB[par][:], func=AF.Square),
                                      wait=[smm_tok.get((oc - 2, 1))], inc="act")
        emit_stats(15)
        t1 = P.op("vector", lambda e: e.tensor_copy(out=MEAN_SB[:], in_=MEAN2_PS[:]),
                  wait=[smm_tok[(15, 0)], gb_lastload.get(1)],
                  inc="dve")
        t2 = P.op("scalar", lambda e: e.activation(out=M2_SB[:], in_=MEAN_SB[:],
                                                   func=AF.Square),
                  wait=[t1, gb_lastload.get(2)], inc="act")
        t3 = P.op("vector", lambda e: e.tensor_tensor(out=VAR_SB[:], in0=MSQ2_PS[:],
                                                      in1=M2_SB[:], op=OP.subtract),
                  wait=[smm_tok[(15, 1)], t2,
                        pa.get("var_last_ld") if pa is not None else None], inc="dve")
        t4 = P.op("scalar", lambda e: e.activation(out=STD_SB[:], in_=VAR_SB[:], func=AF.Ln,
                                                   bias=EPS_SB[:, 0:1]),
                  wait=[t3, gb_lastload.get(3)], inc="act")
        t5 = P.op("scalar", lambda e: e.activation(out=RSTD_SB[:], in_=STD_SB[:],
                                                   func=AF.Exp, scale=-0.5),
                  wait=[t4, pa["xl2"].get(15) if pa is not None else None], inc="act")
        pre_wq = {}
        if th == 0:
            for q in range(2):
                pre_wq[q] = P.op("sync", dma(WSTG[q][:], wq_blk[q]),
                                 wait=[wps_tok[14 + q]], inc=f"sw{q}", n=16)
        if th == 0:
            rslot = [XP_SB[0][:], XP_SB[1][:], XRES[0][:], XRES[1][:], VAR_SB[:]]
            rfree = [cast2_tok[(14, 1)], cast2_tok[(15, 1)], xp_tok[15], xp_tok[15], t5]
        else:
            rslot = [XP_SB[0][:], XP_SB[1][:], XRES[0][:], XRES[1][:], VAR_SB[:],
                     DEN_SB[:], RCP_SB[:]]
            rfree = [cast2_tok[(14, 1)], cast2_tok[(15, 1)], xp_tok[15], xp_tok[15], t5,
                     attn_tok[15], attn_tok[15]]
        actx = {"th": th, "h0": h0, "t5": t5, "kc": 0, "nL": len(rslot),
                "rslot": rslot, "xpb": {}, "xl2": {}, "xl2st": {}}
        if pa is not None:
            actx["xl2st"][-2] = pa["xl2st"].get(14)
            actx["xl2st"][-1] = pa["xl2st"].get(15)
        for s in range(actx["nL"]):
            actx["xpb"][s] = P.op("gpsimd", dma(rslot[s], xp_scr[s][:, h0:h0 + 1024]),
                                  wait=[xp_store_tok[(th, 15)], rfree[s]],
                                  inc=f"bx{th}{s}", n=16)
            if s == 4:
                actx["var_last_ld"] = actx["xpb"][s]
        prev_actx = actx
        prev_th = {
            "pps_free": xp_tok[15],
            "stats_free": t3,
            "qps_free": pmm_tok[15],
            "wstg0": wps_tok[14],
            "wstg1": wps_tok[15],
            "pre_wq0": pre_wq.get(0),
            "pre_wq1": pre_wq.get(1),
            "t3": t3,
            "t5": t5,
        }
        bp_ps.close()

    assert g_state["i"] == n_gsteps

    gf_ps = contextlib.ExitStack()
    GPS = gf_ps.enter_context(nc.psum_tensor("GPS", [1, 512], F32))
    BPS = gf_ps.enter_context(nc.psum_tensor("BPS", [128, 512], F32))
    g_last = g_state["acttok"][n_gsteps - 1]
    rtoks = []
    for m, (c0, c1) in enumerate(((0, 88), (88, 176), (176, 272))):
        rtoks.append(P.op("vector",
                          lambda e, m=m, c0=c0, c1=c1: e.tensor_reduce(
                              out=GSUM[:, m:m + 1], in_=GACC[:, c0:c1], axis=AX, op=OP.add),
                          wait=[g_last], inc="dve"))
    gmm = []
    for m in range(3):
        gmm.append(P.op("tensor",
                        lambda e, m=m: e.matmul(GPS[:, m:m + 1], GSUM[:, m:m + 1], ONES_CF[:],
                                                start=True, stop=True),
                        wait=[rtoks[m], t_cs, prev_th["pps_free"],
                              prev_th["stats_free"]], inc="pe"))
    tg1 = P.op("vector", lambda e: e.tensor_scalar(out=SCLA[:, 0:3], in0=GPS[0:1, 0:3],
                                                   scalar1=1.0 / NW, scalar2=1e-8,
                                                   op0=OP.mult, op1=OP.add),
               wait=[gmm[2]], inc="dve")
    tg2 = P.op("vector", lambda e: e.reciprocal(out=SCLB[:, 0:3], in_=SCLA[:, 0:3]),
               wait=[tg1], inc="dve")
    tg3 = P.op("tensor", lambda e: e.matmul(BPS[:, 0:3], ONESROW[:], SCLB[:, 0:3],
                                            start=True, stop=True),
               wait=[tg2], inc="pe")
    t_scol = P.op("vector", lambda e: e.tensor_copy(out=S_COL[:, 0:3], in_=BPS[:, 0:3]),
                  wait=[tg3], inc="dve")
    gf_ps.close()

    while prev_actx["kc"] < KC:
        emit_apply_chunk(prev_actx)

    P.barrier()
    b_es.close()
    ab_es.close()

    f_es = contextlib.ExitStack()
    sbF = lambda name, shape, dt: f_es.enter_context(nc.sbuf_tensor(name, shape, dt))

    HID = [sbF(f"HID{i}", [128, 1024], BF16) for i in range(JC)]
    XLN2T = [sbF(f"XL2T{i}", [128, 1024], BF16) for i in range(KC)]
    W1S = [sbF(f"W1S{i}", [128, 2048], BF16) for i in range(2)]
    W2S = [sbF(f"W2S{i}", [128, 2048], BF16) for i in range(2)]
    SIG = [sbF(f"SIG{i}", [128, 1024], BF16) for i in range(2)]
    QFA = [sbF(f"QFA{i}", [128, 2048], F32) for i in range(3)]
    QIA = sbF("QIA", [128, 2048], I32)
    W3S = [sbF(f"W3S{i}", [128, 5632], BF16) for i in range(2)]
    OUT_SB = [sbF(f"OUT{i}", [128, 1024], F32) for i in range(2)]

    w3_strips = []
    for fc in range(16):
        for c0 in range(0, 5632, 1024):
            w3_strips.append((fc, c0, min(1024, 5632 - c0)))
    w3pq = {"i": 0, "mm": {}, "mult": {}, "asm": {}, "st": {}, "ommref": {},
            "qfa0free": None}

    def emit_w3_step():
        s = w3pq["i"]
        if s >= len(w3_strips):
            return
        w3pq["i"] = s + 1
        fc, c0, w = w3_strips[s]
        p2 = s % 2
        if s < 12:
            dst = OUT_SB[p2][:, 0:w]
        else:
            dst = QFA[0][:, p2 * 1024:p2 * 1024 + w]
        lw = [w3pq["mm"].get(s - 2)]
        if s in (12, 13):
            lw.append(w3pq["qfa0free"])
        t_ld = P.op("sync", dma(dst, w3_blk[fc][:, c0:c0 + w]), wait=lw,
                    inc=f"q3{p2}", n=16)
        mw = [t_ld, t_scol]
        if c0 == 0 and fc >= 2:
            mw.extend([w3pq["ommref"][fc - 2], w3pq["st"][fc - 2]])
        w3pq["mult"][s] = P.op(
            "vector",
            lambda e, dst=dst, w=w: e.tensor_scalar(
                out=QIA[:, 0:w], in0=dst, scalar1=S_COL[:, 2:3], scalar2=None,
                op0=OP.mult),
            wait=mw, inc="dve")
        w3pq["mm"][s] = P.op(
            "vector",
            lambda e, fc=fc, c0=c0, w=w: e.tensor_scalar(
                out=W3S[fc % 2][:, c0:c0 + w], in0=QIA[:, 0:w],
                scalar1=1.0, scalar2=-1.0, op0=OP.min, op1=OP.max),
            inc="dve")
        if c0 == 5120:
            w3pq["asm"][fc] = w3pq["mm"][s]
            w3pq["st"][fc] = P.op("gpsimd", dma(w3q_scr[fc], W3S[fc % 2][:]),
                                  wait=[w3pq["mm"][s]], inc=f"g3{fc % 2}", n=16)

    out_tok = {}
    prev_tile = {}
    for tile in range(2):
        tc0 = tile * 1024
        xl2t_tok = {}
        for kc in range(KC):
            xl2t_tok[kc] = P.op("sync", dma(XLN2T[kc][:], xl2_scr[kc][:, tc0:tc0 + 1024]),
                                wait=[prev_tile.get("ff143")], inc=f"sf{kc}", n=16)
        f1_ps = contextlib.ExitStack()
        X1PS = [f1_ps.enter_context(nc.psum_tensor(f"X1PS{tile}{i}", [128, 1024], F32))
                for i in range(2)]
        X2PS = [f1_ps.enter_context(nc.psum_tensor(f"X2PS{tile}{i}", [128, 1024], F32))
                for i in range(2)]
        w1s_tok = {}
        w2s_tok = {}
        w1wr_tok = {}
        w2wr_tok = {}
        ff1_tok = {}
        sig_tok = {}
        hid_tok = {}
        qdve_tok = {}

        def emit_hid(jc, sig_tok=sig_tok, hid_tok=hid_tok, X2PS=X2PS):
            par = jc % 2
            hid_tok[jc] = P.op("vector",
                               lambda e, jc=jc, par=par: e.tensor_tensor(
                                   out=HID[jc][:], in0=SIG[par][:], in1=X2PS[par][:],
                                   op=OP.mult),
                               wait=[sig_tok[jc]], inc="dve")

        for jc in range(JC):
            par = jc % 2
            if tile == 0:
                q1b = par * 2
                tq1 = P.op("sync", dma(QFA[q1b][:], w1_blk[jc]),
                           wait=[qdve_tok.get((jc - 2, 0))], inc=f"qa{par}", n=16)
                tq2 = P.op("sync", dma(QFA[1][:], w2_blk[jc]),
                           wait=[qdve_tok.get((jc - 1, 1))], inc="qb", n=16)
                P.op("vector",
                     lambda e, q1b=q1b: e.tensor_scalar(out=QIA[:], in0=QFA[q1b][:],
                                                        scalar1=S_COL[:, 0:1], scalar2=None,
                                                        op0=OP.mult),
                     wait=[tq1, t_scol, ff1_tok.get(jc - 2), w1wr_tok.get(jc - 2)],
                     inc="dve")
                qdve_tok[(jc, 0)] = P.op(
                    "vector",
                    lambda e, par=par: e.tensor_scalar(out=W1S[par][:], in0=QIA[:],
                                                       scalar1=1.0, scalar2=-1.0,
                                                       op0=OP.min, op1=OP.max), inc="dve")
                w1s_tok[jc] = qdve_tok[(jc, 0)]
                w1wr_tok[jc] = P.op("gpsimd", dma(w1q_scr[jc], W1S[par][:]),
                                    wait=[qdve_tok[(jc, 0)]], inc="gw1", n=16)
                P.op("vector",
                     lambda e: e.tensor_scalar(out=QIA[:], in0=QFA[1][:],
                                               scalar1=S_COL[:, 1:2], scalar2=None,
                                               op0=OP.mult),
                     wait=[tq2, w2wr_tok.get(jc - 2)], inc="dve")
                qdve_tok[(jc, 1)] = P.op(
                    "vector",
                    lambda e, par=par: e.tensor_scalar(out=W2S[par][:], in0=QIA[:],
                                                       scalar1=1.0, scalar2=-1.0,
                                                       op0=OP.min, op1=OP.max), inc="dve")
                w2s_tok[jc] = qdve_tok[(jc, 1)]
                w2wr_tok[jc] = P.op("gpsimd", dma(w2q_scr[jc], W2S[par][:]),
                                    wait=[qdve_tok[(jc, 1)]], inc="gw2", n=16)
                if 2 <= jc < 14:
                    emit_w3_step()
            else:
                wL = [ff1_tok.get(jc - 2)]
                if jc < 2:
                    wL.extend([prev_tile.get("ff143"), prev_tile.get("w1wr"),
                               prev_tile.get("w2wr")])
                P.op("sync", dma(W1S[par][:], w1q_scr[jc]),
                     wait=wL, inc=f"sL{par}", n=16)
                tl2 = P.op("sync", dma(W2S[par][:], w2q_scr[jc]), inc=f"sL{par}", n=16)
                w1s_tok[jc] = tl2
                w2s_tok[jc] = tl2
            for kc in range(KC):
                for h2 in range(2):
                    w1w = [w1s_tok[jc], xl2t_tok[kc],
                           sig_tok.get(jc - 2), hid_tok.get(jc - 2)]
                    if jc == 0 and kc == 0 and h2 == 0:
                        w1w.append(prev_tile.get("evac15"))
                    P.op("tensor",
                         lambda e, kc=kc, h2=h2, par=par: e.matmul(
                             X1PS[par][:, h2 * 512:(h2 + 1) * 512],
                             W1S[par][:, kc * 128:(kc + 1) * 128],
                             XLN2T[kc][:, h2 * 512:(h2 + 1) * 512],
                             start=(kc == 0), stop=(kc == KC - 1)),
                         wait=w1w, inc="pe")
                    ff1_tok[jc] = P.op(
                        "tensor",
                        lambda e, kc=kc, h2=h2, par=par: e.matmul(
                            X2PS[par][:, h2 * 512:(h2 + 1) * 512],
                            W2S[par][:, kc * 128:(kc + 1) * 128],
                            XLN2T[kc][:, h2 * 512:(h2 + 1) * 512],
                            start=(kc == 0), stop=(kc == KC - 1)),
                        wait=[w2s_tok[jc]], inc="pe")
            sig_tok[jc] = P.op("scalar",
                               lambda e, par=par: e.activation(out=SIG[par][:],
                                                               in_=X1PS[par][:],
                                                               func=AF.Sigmoid),
                               wait=[ff1_tok[jc]], inc="act")
            if jc > 0:
                emit_hid(jc - 1)
        emit_hid(JC - 1)
        if DBG and tile == 0:
            for jc in range(JC):
                P.op("gpsimd", dma(hid_dbg[jc], HID[jc][:]), wait=[hid_tok[jc]],
                     inc="gs", n=16)
        f1_ps.close()

        if tile == 0:
            w3pq["qfa0free"] = qdve_tok[(42, 0)]
        f2_ps = contextlib.ExitStack()
        OPS = [f2_ps.enter_context(nc.psum_tensor(f"OPS{tile}{i}", [128, 1024], F32))
               for i in range(4)]
        w3s_tok = {}
        omm_tok = {}
        xps_tok = {}
        tout_tok = {}

        def emit_out(fc, omm_tok=omm_tok, xps_tok=xps_tok, tout_tok=tout_tok,
                     OPS=OPS, tile=tile, tc0=tc0):
            par = fc % 2
            wout = [omm_tok[fc], xps_tok[fc], out_tok.get((tile, fc - 2))]
            if tile == 0 and fc < 2:
                wout.append(w3pq["mult"][10 + fc])
            if tile == 1 and fc < 2:
                wout.append(out_tok.get((0, 14 + fc)))
            tout_tok[fc] = P.op("vector",
                                lambda e, par=par, fc=fc: e.tensor_tensor(
                                    out=OUT_SB[par][:], in0=OPS[fc % 4][:],
                                    in1=QFA[1 + par][:, 0:1024], op=OP.add),
                                wait=wout, inc="dve")
            out_tok[(tile, fc)] = P.op("gpsimd",
                                       dma(y[fc * 128:(fc + 1) * 128, tc0:tc0 + 1024],
                                           OUT_SB[par][:]),
                                       wait=[tout_tok[fc]], inc=f"gy{par}", n=16)

        for fc in range(16):
            par = fc % 2
            if tile == 0:
                w3s_tok[fc] = w3pq["asm"][fc]
            else:
                w3 = [omm_tok.get(fc - 2), w3pq["st"][fc]]
                if fc < 2:
                    w3.append(prev_tile.get("omm14" if fc == 0 else "omm15"))
                w3s_tok[fc] = P.op("sync", dma(W3S[par][:], w3q_scr[fc]),
                                   wait=w3, inc=f"s3{par}", n=16)
            wxps = [tout_tok.get(fc - 2)]
            if tile == 0 and fc < 2:
                wxps.append(qdve_tok.get((JC - 1, 1 - par)))
            xps_tok[fc] = P.op("sync",
                               dma(QFA[1 + par][:, 0:1024], xp_scr[fc][:, tc0:tc0 + 1024]),
                               wait=wxps, inc=f"sp{par}", n=16)
            for jc in range(JC):
                for h2 in range(2):
                    wo = [w3s_tok[fc], hid_tok[JC - 1],
                          tout_tok.get(fc - 4)]
                    if fc == 0 and jc == 0:
                        wo.append(sig_tok[JC - 1])
                    omm_tok[fc] = P.op(
                        "tensor",
                        lambda e, jc=jc, h2=h2, par=par, fc=fc: e.matmul(
                            OPS[fc % 4][:, h2 * 512:(h2 + 1) * 512],
                            W3S[par][:, jc * 128:(jc + 1) * 128],
                            HID[jc][:, h2 * 512:(h2 + 1) * 512],
                            start=(jc == 0), stop=(jc == JC - 1)),
                        wait=wo, inc="pe")
            if fc > 0:
                emit_out(fc - 1)
            if tile == 0 and fc < 14:
                w3pq["ommref"][fc] = omm_tok[fc]
                for _ in range(6):
                    emit_w3_step()
        emit_out(15)
        prev_tile = {"omm14": omm_tok.get(14), "omm15": omm_tok.get(15),
                     "w1wr": w1wr_tok.get(JC - 1), "w2wr": w2wr_tok.get(JC - 1),
                     "ff143": ff1_tok.get(JC - 1),
                     "evac15": tout_tok.get(15)}
        f2_ps.close()

    f_es.close()
    es.close()

    sem_es = contextlib.ExitStack()
    sems = {s: sem_es.enter_context(nc.semaphore(s)) for s in sorted(P.cnt)}
    with sem_es:
        with nc.Block() as block:
            def runner(name):
                def f(e):
                    for wl, fn, inc, n in P.ops[name]:
                        for (s, v) in wl:
                            e.wait_ge(sems[s], v)
                        if fn is not None:
                            ins = fn(e)
                            if inc is not None:
                                ins.then_inc(sems[inc], n)
                    if name in ("gpsimd", "sync"):
                        for s, v in P.cnt.items():
                            e.wait_ge(sems[s], v)
                return f
            block.sync(runner("sync"))
            block.gpsimd(runner("gpsimd"))
            block.tensor(runner("tensor"))
            block.scalar(runner("scalar"))
            block.vector(runner("vector"))
    return nc


def host_inputs(L0_attn_x, L0_attn_w_qkv, L0_attn_w_proj, L0_attn_b_proj,
                L0_ffn_w1, L0_ffn_w2, L0_ffn_w3):
    wq = np.asarray(L0_attn_w_qkv, np.float32)
    WqT = wq[0:2048].T
    K = wq[2048:4096].T
    V = wq[4096:6144].T
    wq_blk = np.ascontiguousarray(
        WqT.reshape(16, 128, 16, 128).transpose(2, 1, 0, 3)).reshape(16, 128, 2048)
    slabs = []
    for gg in range(8):
        g0, g1 = 2 * gg, 2 * gg + 1
        slab = np.concatenate([K[:, g0 * 128:(g0 + 1) * 128], K[:, g1 * 128:(g1 + 1) * 128],
                               V[:, g0 * 128:(g0 + 1) * 128], V[:, g1 * 128:(g1 + 1) * 128]],
                              axis=1)
        slabs.append(np.ascontiguousarray(slab).reshape(16, 128, 512))
    wkv_blk = np.ascontiguousarray(np.stack(slabs))
    WpT = np.asarray(L0_attn_w_proj, np.float32).T
    wpj_blk = np.ascontiguousarray(
        WpT.reshape(16, 128, 16, 128).transpose(2, 1, 0, 3)).reshape(16, 128, 2048)
    W1T = np.asarray(L0_ffn_w1, np.float32).T
    w1_blk = np.ascontiguousarray(
        W1T.reshape(16, 128, 44, 128).transpose(2, 1, 0, 3)).reshape(44, 128, 2048)
    W2T = np.asarray(L0_ffn_w2, np.float32).T
    w2_blk = np.ascontiguousarray(
        W2T.reshape(16, 128, 44, 128).transpose(2, 1, 0, 3)).reshape(44, 128, 2048)
    W3T = np.asarray(L0_ffn_w3, np.float32).T
    w3_blk = np.ascontiguousarray(
        W3T.reshape(44, 128, 16, 128).transpose(2, 1, 0, 3)).reshape(16, 128, 5632)
    bcol = np.ascontiguousarray(np.asarray(L0_attn_b_proj, np.float32).reshape(16, 128).T)
    ones_stat = np.full((128, 128), 1.0 / 2048.0, np.float32)
    ones_col = np.ones((128, 1), np.float32)
    onesrow = np.ones((1, 128), np.float32)
    mask = np.zeros((128, 128), np.float32)
    for h in range(4):
        mask[h * 32:(h + 1) * 32, h * 32:(h + 1) * 32] = 1.0
    epss = np.zeros((128, 2), np.float32)
    epss[:, 0] = 1e-5
    epss[:, 1] = 1e-15
    shared = dict(wq_blk=wq_blk, wkv_blk=wkv_blk, wpj_blk=wpj_blk, w1_blk=w1_blk,
                  w2_blk=w2_blk, w3_blk=w3_blk, bcol=bcol, ones_stat=ones_stat,
                  ones_col=ones_col, onesrow=onesrow, mask_bd=mask, epss=epss)
    x = np.asarray(L0_attn_x, np.float32)
    in_maps = []
    for b in range(8):
        m = dict(shared)
        m["xfm"] = np.ascontiguousarray(x[b].T)
        in_maps.append(m)
    return in_maps


def kernel(L0_attn_x, L0_attn_w_qkv, L0_attn_w_proj, L0_attn_b_proj,
           L0_ffn_w1, L0_ffn_w2, L0_ffn_w3):
    in_maps = host_inputs(L0_attn_x, L0_attn_w_qkv, L0_attn_w_proj, L0_attn_b_proj,
                          L0_ffn_w1, L0_ffn_w2, L0_ffn_w3)
    nc = build()
    res = run_bass_kernel_spmd(nc, in_maps, core_ids=list(range(8)))
    out = np.empty((8, T, D), np.float32)
    for b in range(8):
        out[b] = np.asarray(res.results[b]["y"]).T
    return out
